# version 15
# baseline (speedup 1.0000x reference)
import math
import numpy as np
import ml_dtypes
from contextlib import ExitStack

import concourse.bass as bass
import concourse.mybir as mybir
from concourse.bass_utils import run_bass_kernel_spmd

F32 = mybir.dt.float32
BF16 = mybir.dt.bfloat16
AF = mybir.ActivationFunctionType
ALU = mybir.AluOpType
AX = mybir.AxisListType

P = 128
D = 2048
T = 2048
DC = D // P
TCW = 512
NTC = T // TCW
NTT = T // P
EPS = 1e-5
ISQ = 1.0 / math.sqrt(128.0)
NEG = -1.0e30

H0, KV0, DH = 16, 4, 128
META, WIN = 128, 1024
TK0 = META + WIN
I0 = 8192
IC0 = I0 // P
I1 = 5632
IC1 = I1 // P

BF = ml_dtypes.bfloat16
NCORES = 8



def _rope_perm():
    return np.concatenate([np.arange(0, DH, 2), np.arange(1, DH, 2)])


def _perm24():
    c = np.arange(I0 // P)[:, None]
    p = np.arange(P)[None, :]
    return (4 * ((c // 4) * P + p) + (c % 4)).reshape(-1)


def prep_w(Wf, S):
    Din, Dout = Wf.shape
    cin, ns = Din // P, Dout // S
    return np.ascontiguousarray(
        Wf.reshape(cin, P, ns, S).transpose(2, 1, 0, 3).reshape(ns, P, cin * S)
    ).astype(np.float32)


def rope_tables():
    inv = 1.0 / (10000.0 ** (np.arange(0, DH, 2, dtype=np.float64) / DH))
    ang = np.arange(T, dtype=np.float64)[None, :] * inv[:, None]
    return (np.cos(ang).astype(np.float32), np.sin(ang).astype(np.float32))


def col_perm_heads(Wf, perm, dh=DH):
    Dout = Wf.shape[1]
    nh = Dout // dh
    return np.ascontiguousarray(
        Wf.reshape(Wf.shape[0], nh, dh)[:, :, perm].reshape(Wf.shape[0], Dout))


def bias_cols(b):
    n = b.shape[0] // P
    return np.ascontiguousarray(b.reshape(n, P).T).astype(np.float32)



class PB:
    SEMS = ("gq", "sq", "pe", "dve", "act", "sf")

    def __init__(self, nc, es, npb=4, nwb=0):
        self.nc = nc
        self.es = es
        self.blk = es.enter_context(nc.Block())
        self.sem = {}
        self.cnt = {}
        for s in self.SEMS:
            self.sem[s] = es.enter_context(nc.semaphore("sem_" + s))
            self.cnt[s] = 0
        self.scopes = [es]
        self.prefix = ""
        self.pbank = []
        self.wbuf = []
        if npb or nwb:
            self.alloc_pool(npb, nwb)

    def sem_get(self, name):
        if name not in self.sem:
            self.sem[name] = self.es.enter_context(self.nc.semaphore("sem_" + name))
            self.cnt[name] = 0
        return self.sem[name]

    def sb(self, name, shape, dt):
        return self.scopes[-1].enter_context(
            self.nc.sbuf_tensor(self.prefix + name, shape, dt))

    def ps(self, name, shape, dt=F32):
        return self.scopes[-1].enter_context(
            self.nc.psum_tensor(self.prefix + name, shape, dt))

    def alloc_pool(self, npb, nwb):
        self.pbank = [self.ps(f"pbank{i}", [P, TCW], F32) for i in range(npb)]
        self.wbuf = [self.sb(f"wbuf{i}", [P, 8192], BF16) for i in range(nwb)]

    def push(self, prefix):
        self.scopes.append(ExitStack())
        self.prefix = prefix
        return self.scopes[-1]

    def pop(self):
        self.scopes.pop().close()
        self.pbank = []
        self.wbuf = []

    def inc(self, ins, s, n=1):
        ins.then_inc(self.sem[s], n)
        self.cnt[s] += n
        return self.cnt[s]

    def dma(self, eng, dst, src, sem=None):
        if sem is None:
            sem = "gq" if eng is self.nc.gpsimd else "sq"
        else:
            self.sem_get(sem)
        ins = eng.dma_start(out=dst, in_=src)
        return self.inc(ins, sem, 16)

    def wait(self, eng, s, tok):
        if tok > 0:
            eng.wait_ge(self.sem[s], tok)

    def self_edge(self, eng, ins):
        tok = self.inc(ins, "sf")
        eng.wait_ge(self.sem["sf"], tok)

    def barrier(self):
        snap = dict(self.cnt)
        for bname, own in (("gpsimd", "gq"), ("sync", "sq"), ("tensor", "pe"),
                           ("vector", "dve"), ("scalar", "act")):
            def mk(own_s):
                def f(eng):
                    for s, v in snap.items():
                        if s != own_s and s != "sf" and v > 0:
                            eng.wait_ge(self.sem[s], v)
                return f
            getattr(self.blk, bname)(mk(own))

    def wait_all_dma(self, eng):
        for s, v in self.cnt.items():
            if s not in ("pe", "dve", "act", "sf") and v > 0:
                eng.wait_ge(self.sem[s], v)



def mk_ln_bufs(pb, ping=True):
    lb = {}
    lb["ping"] = ping
    lb["xb"] = pb.sb("ln_xb", [P, (2 if ping else 1) * DC * TCW], BF16)
    lb["sq"] = pb.sb("ln_sq", [P, DC * TCW], BF16)
    lb["onesb"] = pb.sb("ln_ones", [P, 1], BF16)
    lb["ones_row"] = pb.sb("ln_onesrow", [1, P], F32)
    for k, shape, dt in (("m", [1, TCW], F32), ("v", [1, TCW], F32),
                         ("r", [1, TCW], F32), ("a", [1, TCW], F32),
                         ("t", [1, TCW], F32), ("rb", [P, TCW], BF16),
                         ("ab", [P, TCW], BF16), ("t2", [P, TCW], BF16)):
        lb[k] = pb.sb("ln_" + k, shape, dt)
    return lb


def emit_ln(pb, x_dram, lnb, tcs, xn_ap):
    xb, sq = lnb["xb"], lnb["sq"]
    png = 1 if lnb.get("ping", True) else 0
    m_sb, v_sb, r_sb, a_sb, t_sb = (lnb[k] for k in ("m", "v", "r", "a", "t"))
    rb, ab, t2 = lnb["rb"], lnb["ab"], lnb["t2"]

    pb.sem_get("lnx0")
    pb.sem_get("lnx1")
    gq0p = [pb.cnt["lnx0"], pb.cnt["lnx1"]]
    pe0 = pb.cnt["pe"]
    act0 = pb.cnt["act"]
    dve0 = pb.cnt["dve"]
    n = len(tcs)
    gq_tok = [gq0p[k % 2] + 256 * (k // 2 + 1) for k in range(n)]
    act_sq = [act0 + 2 * k + 1 for k in range(n)]
    act_std = [act0 + 2 * k + 2 for k in range(n)]
    pe_sums = [pe0 + 2 * k + 1 for k in range(n)]
    pe_bc = [pe0 + 2 * k + 2 for k in range(n)]
    dve_var = [dve0 + 5 * k + 3 for k in range(n)]
    dve_r = [dve0 + 5 * k + 4 for k in range(n)]
    dve_xn = [dve0 + 5 * k + 5 for k in range(n)]

    @pb.blk.gpsimd
    def _(g):
        for k, tc in enumerate(tcs):
            if k >= 2 - (1 - png):
                g.wait_ge(pb.sem["dve"], dve_xn[k - 2 + (1 - png)])
            half = (k % 2) * DC * TCW * png
            for c in range(DC):
                tok = pb.dma(g, xb[:, half + c * TCW: half + (c + 1) * TCW],
                             x_dram[c][:, tc * TCW:(tc + 1) * TCW],
                             sem=f"lnx{k % 2}")
            assert tok == gq_tok[k]

    @pb.blk.scalar
    def _(sc):
        for k, tc in enumerate(tcs):
            sc.wait_ge(pb.sem[f"lnx{k % 2}"], gq_tok[k])
            if k >= 1:
                sc.wait_ge(pb.sem["pe"], pe_sums[k - 1])
            half = (k % 2) * DC * TCW * png
            ins = sc.activation(sq[:, :], xb[:, half: half + DC * TCW], AF.Square)
            tok = pb.inc(ins, "act")
            assert tok == act_sq[k]
            sc.wait_ge(pb.sem["dve"], dve_var[k])
            ins = sc.activation(t_sb[0:1, :], v_sb[0:1, :], AF.Sqrt)
            tok = pb.inc(ins, "act")
            assert tok == act_std[k]

    @pb.blk.tensor
    def _(te):
        for k, tc in enumerate(tcs):
            te.wait_ge(pb.sem["act"], act_sq[k])
            if k >= 1:
                te.wait_ge(pb.sem["dve"], dve_var[k - 1])
            half = (k % 2) * DC * TCW * png
            for c in range(DC):
                te.matmul(pb.pbank[0][0:1, :], lnb["onesb"][:, 0:1],
                          xb[:, half + c * TCW: half + (c + 1) * TCW],
                          start=(c == 0), stop=(c == DC - 1))
            for c in range(DC):
                ins = te.matmul(pb.pbank[1][0:1, :], lnb["onesb"][:, 0:1],
                                sq[:, c * TCW:(c + 1) * TCW],
                                start=(c == 0), stop=(c == DC - 1))
            tok = pb.inc(ins, "pe")
            assert tok == pe_sums[k]
            te.wait_ge(pb.sem["dve"], dve_r[k])
            if k >= 1:
                te.wait_ge(pb.sem["dve"], dve_xn[k - 1])
            te.matmul(pb.pbank[2][:, :], lnb["ones_row"][0:1, :], r_sb[0:1, :],
                      start=True, stop=True)
            ins = te.matmul(pb.pbank[3][:, :], lnb["ones_row"][0:1, :], a_sb[0:1, :],
                            start=True, stop=True)
            tok = pb.inc(ins, "pe")
            assert tok == pe_bc[k]

    @pb.blk.vector
    def _(v):
        for k, tc in enumerate(tcs):
            v.wait_ge(pb.sem["pe"], pe_sums[k])
            ins = v.tensor_scalar_mul(m_sb[0:1, :], pb.pbank[0][0:1, :], 1.0 / D)
            v.wait_ge(pb.sem["dve"], pb.inc(ins, "dve"))
            ins = v.tensor_tensor(t_sb[0:1, :], m_sb[0:1, :], m_sb[0:1, :], ALU.mult)
            v.wait_ge(pb.sem["dve"], pb.inc(ins, "dve"))
            v.scalar_tensor_tensor(v_sb[0:1, :], pb.pbank[1][0:1, :], 1.0 / D,
                                   t_sb[0:1, :], ALU.mult, ALU.subtract)
            ins = v.tensor_scalar_add(v_sb[0:1, :], v_sb[0:1, :], EPS)
            tok = pb.inc(ins, "dve")
            assert tok == dve_var[k]
            v.wait_ge(pb.sem["act"], act_std[k])
            pb.self_edge(v, v.reciprocal(r_sb[0:1, :], t_sb[0:1, :]))
            ins = v.tensor_tensor(a_sb[0:1, :], m_sb[0:1, :], r_sb[0:1, :], ALU.mult)
            tok = pb.inc(ins, "dve")
            assert tok == dve_r[k]
            v.wait_ge(pb.sem["pe"], pe_bc[k])
            v.tensor_copy(rb[:, :], pb.pbank[2][:, :])
            v.tensor_copy(ab[:, :], pb.pbank[3][:, :])
            half = (k % 2) * DC * TCW * png
            for c in range(DC):
                v.tensor_tensor(t2[:, :], xb[:, half + c * TCW: half + (c + 1) * TCW],
                                rb[:, :], ALU.mult)
                ins = v.tensor_tensor(xn_ap(c, tc), t2[:, :], ab[:, :], ALU.subtract)
            tok = pb.inc(ins, "dve")
            assert tok == dve_xn[k]



def emit_proj_fm(pb, wslices, n_slices, cin, S, rhs_ap, evac_emit, ev_guard=None,
                 nwb=4):
    osubs = S // P
    gps = osubs * NTC
    pe0 = pb.cnt["pe"]
    ev0 = pb.cnt["act"]
    NB = 4
    gq_toks = []

    @pb.blk.gpsimd
    def _(g):
        for s in range(n_slices):
            if s >= nwb:
                g.wait_ge(pb.sem["pe"], pe0 + (s - nwb + 1) * gps)
            tok = pb.dma(g, pb.wbuf[s % nwb][:, 0:cin * S], wslices[s],
                         sem=f"w{s % nwb}")
            gq_toks.append(tok)

    groups = [(s, osub, tc) for s in range(n_slices) for osub in range(osubs)
              for tc in range(NTC)]

    @pb.blk.tensor
    def _(te):
        for gi, (s, osub, tc) in enumerate(groups):
            if osub == 0 and tc == 0:
                te.wait_ge(pb.sem_get(f"w{s % nwb}"), gq_toks[s])
            bank = pb.pbank[gi % NB]
            if gi >= NB:
                te.wait_ge(pb.sem["act"], ev0 + (gi - NB + 1))
            for c in range(cin):
                ins = te.matmul(bank[:, :],
                                pb.wbuf[s % nwb][:, c * S + osub * P: c * S + osub * P + P],
                                rhs_ap(c, tc), start=(c == 0), stop=(c == cin - 1))
            pb.inc(ins, "pe")

    @pb.blk.scalar
    def _(sc):
        for gi, (s, osub, tc) in enumerate(groups):
            sc.wait_ge(pb.sem["pe"], pe0 + gi + 1)
            if ev_guard is not None:
                ev_guard(sc, gi)
            ins = evac_emit(sc, gi, s, osub, tc, pb.pbank[gi % NB])
            pb.inc(ins, "act")

    return groups, ev0


def emit_rope_store(pb, groups, ev_tok_fn, cosT, sinT, ev, rA, rB, st, store_dst_fn):
    dve0 = pb.cnt["dve"]
    pb.sem_get("y0")
    pb.sem_get("y1")
    y0 = [pb.cnt["y0"], pb.cnt["y1"]]

    @pb.blk.vector
    def _(v):
        for gi, g in enumerate(groups):
            tc = g[2]
            e = ev[gi % 2]
            d = st[gi % 2]
            v.wait_ge(pb.sem["act"], ev_tok_fn(gi))
            if gi >= 2:
                pb.wait(v, f"y{gi % 2}", y0[gi % 2] + 16 * (gi // 2))
            csl = cosT[0:64, tc * TCW:(tc + 1) * TCW]
            csh = cosT[64:128, tc * TCW:(tc + 1) * TCW]
            snl = sinT[0:64, tc * TCW:(tc + 1) * TCW]
            snh = sinT[64:128, tc * TCW:(tc + 1) * TCW]
            v.tensor_tensor(rA[:, :], e[0:64, :], csl, ALU.mult)
            v.tensor_tensor(rB[:, :], e[64:128, :], snh, ALU.mult)
            v.tensor_tensor(d[0:64, :], rA[:, :], rB[:, :], ALU.subtract)
            v.tensor_tensor(rA[:, :], e[0:64, :], snl, ALU.mult)
            v.tensor_tensor(rB[:, :], e[64:128, :], csh, ALU.mult)
            ins = v.tensor_tensor(d[64:128, :], rA[:, :], rB[:, :], ALU.add)
            pb.inc(ins, "dve")

    @pb.blk.sync
    def _(sy):
        for gi, g in enumerate(groups):
            sy.wait_ge(pb.sem["dve"], dve0 + gi + 1)
            pb.dma(sy, store_dst_fn(g), st[gi % 2][:, :], sem=f"y{gi % 2}")
    return dve0



def emit_resid_stream(pb, ngroups, pe0, bank_fn, x_src_fn, xo_dst_fn, dve_op_fn,
                      xt, xo):
    dve0 = pb.cnt["dve"]
    nxt = len(xt)
    for i in range(nxt):
        pb.sem_get(f"x{i}")
    for i in range(2):
        pb.sem_get(f"y{i}")
    ld_toks = [None] * ngroups
    st_toks = [None] * ngroups

    @pb.blk.sync
    def _(sy):
        for g in range(ngroups):
            if g >= nxt:
                sy.wait_ge(pb.sem["dve"], dve0 + g - nxt + 1)
            ld_toks[g] = pb.dma(sy, xt[g % nxt][:, :], x_src_fn(g), sem=f"x{g % nxt}")
            if g >= 2:
                sy.wait_ge(pb.sem["dve"], dve0 + g - 1)
                st_toks[g - 2] = pb.dma(sy, xo_dst_fn(g - 2), xo[(g - 2) % 2][:, :],
                                        sem=f"y{(g - 2) % 2}")
        for j in (ngroups - 2, ngroups - 1):
            sy.wait_ge(pb.sem["dve"], dve0 + j + 1)
            st_toks[j] = pb.dma(sy, xo_dst_fn(j), xo[j % 2][:, :], sem=f"y{j % 2}")

    @pb.blk.vector
    def _(v):
        for g in range(ngroups):
            v.wait_ge(pb.sem["pe"], pe0 + g + 1)
            v.wait_ge(pb.sem[f"x{g % nxt}"], ld_toks[g])
            if g >= 2:
                v.wait_ge(pb.sem[f"y{g % 2}"], st_toks[g - 2])
            ins = dve_op_fn(v, g, xo[g % 2], bank_fn(g), xt[g % nxt])
            pb.inc(ins, "dve")
    return dve0



def emit_qkv(pb, layer, io):
    x_d = io["x"]
    wq_d, wk_d, wv_d = io["wq"], io["wk"], io["wv"]
    onesb_d, onesrow_d = io["c_onesb"], io["c_onesrow"]
    cos_d, sin_d = io["c_cos"], io["c_sin"]
    q_d, k_d, v_d = io["q"], io["k"], io["v"]
    if layer == 1:
        bq_d, bk_d, bv_d = io["bq"], io["bk"], io["bv"]
    k_slices = 1 if layer == 0 else 4
    if True:
        pb.alloc_pool(4, 3)
        lnb = mk_ln_bufs(pb)
        xn = pb.sb("xn", [P, DC * T], BF16)
        cosT = pb.sb("cosT", [P, T], BF16)
        sinT = pb.sb("sinT", [P, T], BF16)
        ev = [pb.sb(f"ev{i}", [P, TCW], BF16) for i in range(2)]
        rA = pb.sb("rA", [64, TCW], BF16)
        rB = pb.sb("rB", [64, TCW], BF16)
        st = [pb.sb(f"st{i}", [P, TCW], BF16) for i in range(2)]
        if layer == 1:
            bq_sb = pb.sb("bq_s", [P, 16], F32)
            bk_sb = pb.sb("bk_s", [P, 16], F32)
            bv_sb = pb.sb("bv_s", [1, D], F32)
            bvb = pb.sb("bvb", [P, D], F32)

        @pb.blk.sync
        def _(sy):
            pb.dma(sy, lnb["onesb"][:, 0:1], onesb_d)
            pb.dma(sy, lnb["ones_row"][0:1, :], onesrow_d)
            pb.dma(sy, cosT[:, :], cos_d)
            pb.dma(sy, sinT[:, :], sin_d)
            if layer == 1:
                pb.dma(sy, bq_sb[:, :], bq_d)
                pb.dma(sy, bk_sb[:, :], bk_d)
                pb.dma(sy, bv_sb[0:1, :], bv_d)
        pb.barrier()

        emit_ln(pb, x_d, lnb, list(range(NTC)),
                lambda c, tc: xn[:, c * T + tc * TCW: c * T + (tc + 1) * TCW])
        pb.barrier()

        def rhs_ap(c, tc):
            return xn[:, c * T + tc * TCW: c * T + (tc + 1) * TCW]

        def mk_qk_evac(bias_sb):
            def _evac(eng, gi, s, osub, tc, ps):
                head = s * 4 + osub
                if bias_sb is not None:
                    return eng.activation(ev[gi % 2][:, :], ps[:, :], AF.Identity,
                                          bias=bias_sb[:, head:head + 1])
                return eng.copy(ev[gi % 2][:, :], ps[:, :])
            return _evac

        rope_base = pb.cnt["dve"]

        def ev_guard(eng, gi):
            pb.wait(eng, "dve", rope_base + gi - 1)

        groups, ev0 = emit_proj_fm(pb, wq_d, 4, DC, TCW, rhs_ap,
                                   mk_qk_evac(bq_sb if layer == 1 else None),
                                   ev_guard=ev_guard, nwb=3)
        emit_rope_store(pb, groups, lambda gi: ev0 + gi + 1, cosT, sinT, ev, rA, rB,
                        st, lambda g: q_d[g[2], g[0] * 4 + g[1]])
        pb.barrier()

        rope_base = pb.cnt["dve"]
        groups, ev0 = emit_proj_fm(pb, wk_d, k_slices, DC, TCW, rhs_ap,
                                   mk_qk_evac(bk_sb if layer == 1 else None),
                                   ev_guard=ev_guard, nwb=3)
        emit_rope_store(pb, groups, lambda gi: ev0 + gi + 1, cosT, sinT, ev, rA, rB,
                        st, lambda g: k_d[g[0] * 4 + g[1]][:, g[2] * TCW:(g[2] + 1) * TCW])
        pb.barrier()

        if layer == 1:
            pe0b = pb.cnt["pe"]
            dve0b = pb.cnt["dve"]

            @pb.blk.tensor
            def _(te):
                for sdv in range(4):
                    if sdv >= 1:
                        te.wait_ge(pb.sem["dve"], dve0b + sdv)
                    ins = te.matmul(pb.pbank[0][:, :], lnb["ones_row"][0:1, :],
                                    bv_sb[0:1, sdv * TCW:(sdv + 1) * TCW],
                                    start=True, stop=True)
                    pb.inc(ins, "pe")

            @pb.blk.vector
            def _(v):
                for sdv in range(4):
                    v.wait_ge(pb.sem["pe"], pe0b + sdv + 1)
                    ins = v.tensor_copy(bvb[:, sdv * TCW:(sdv + 1) * TCW],
                                        pb.pbank[0][:, :])
                    pb.inc(ins, "dve")
            pb.barrier()

        n_dv = 1 if layer == 0 else 4
        dve_v0 = pb.cnt["dve"]
        pe_v0 = pb.cnt["pe"]
        for sdv in range(n_dv):
            pe0 = pb.cnt["pe"]
            dve0v = pb.cnt["dve"]
            tok_wv = [None]

            @pb.blk.gpsimd
            def _(g):
                pb.wait(g, "pe", pe_v0 + NTT * (sdv - 1))
                tok_wv[0] = pb.dma(g, pb.wbuf[sdv % 2][:, 0:DC * TCW],
                                   wv_d if layer == 0 else wv_d[sdv],
                                   sem=f"w{sdv % 2}")

            @pb.blk.tensor
            def _(te):
                te.wait_ge(pb.sem[f"w{sdv % 2}"], tok_wv[0])
                for tt in range(NTT):
                    bank = pb.pbank[tt % 4]
                    Gg = sdv * NTT + tt
                    if Gg >= 4:
                        te.wait_ge(pb.sem["dve"], dve_v0 + Gg - 3)
                    for c in range(DC):
                        ins = te.matmul(bank[:, :],
                                        xn[:, c * T + tt * P: c * T + (tt + 1) * P],
                                        pb.wbuf[sdv % 2][:, c * TCW:(c + 1) * TCW],
                                        start=(c == 0), stop=(c == DC - 1))
                    pb.inc(ins, "pe")

            pb.sem_get("y0")
            pb.sem_get("y1")
            y0v = [pb.cnt["y0"], pb.cnt["y1"]]

            @pb.blk.vector
            def _(v):
                for tt in range(NTT):
                    v.wait_ge(pb.sem["pe"], pe0 + tt + 1)
                    ytok = y0v[tt % 2] + 16 * (tt // 2) if tt >= 2 else y0v[tt % 2]
                    pb.wait(v, f"y{tt % 2}", ytok)
                    if layer == 1:
                        ins = v.tensor_tensor(st[tt % 2][:, :], pb.pbank[tt % 4][:, :],
                                              bvb[:, sdv * TCW:(sdv + 1) * TCW], ALU.add)
                    else:
                        ins = v.tensor_copy(st[tt % 2][:, :], pb.pbank[tt % 4][:, :])
                    pb.inc(ins, "dve")

            @pb.blk.sync
            def _(sy):
                for tt in range(NTT):
                    sy.wait_ge(pb.sem["dve"], dve0v + tt + 1)
                    if layer == 0:
                        pb.dma(sy, v_d[tt], st[tt % 2][:, :], sem=f"y{tt % 2}")
                    else:
                        pb.dma(sy, v_d[tt][:, sdv * TCW:(sdv + 1) * TCW],
                               st[tt % 2][:, :], sem=f"y{tt % 2}")


def io_qkv(nc, layer, pfx="", scratch=()):
    nh_k = KV0 if layer == 0 else 16
    k_slices = 1 if layer == 0 else 4
    vdw = TCW if layer == 0 else D
    kind = lambda n: "Internal" if n in scratch else (
        "ExternalOutput" if n in ("q", "k", "v") else "ExternalInput")
    io = {}
    io["x"] = nc.dram_tensor(pfx + "x", [DC, P, T], F32, kind=kind("x")).ap()
    io["wq"] = nc.dram_tensor(pfx + "wq", [4, P, DC * TCW], F32, kind="ExternalInput").ap()
    io["wk"] = nc.dram_tensor(pfx + "wk", [k_slices, P, DC * TCW], F32, kind="ExternalInput").ap()
    io["wv"] = nc.dram_tensor(pfx + "wv", [P, DC * TCW] if layer == 0 else [4, P, DC * TCW],
                              F32, kind="ExternalInput").ap()
    io["q"] = nc.dram_tensor(pfx + "q", [NTC, 16, P, TCW], BF16, kind=kind("q")).ap()
    io["k"] = nc.dram_tensor(pfx + "k", [nh_k, P, T], BF16, kind=kind("k")).ap()
    io["v"] = nc.dram_tensor(pfx + "v", [NTT, P, vdw], BF16, kind=kind("v")).ap()
    if layer == 1:
        io["bq"] = nc.dram_tensor(pfx + "bq", [P, 16], F32, kind="ExternalInput").ap()
        io["bk"] = nc.dram_tensor(pfx + "bk", [P, 16], F32, kind="ExternalInput").ap()
        io["bv"] = nc.dram_tensor(pfx + "bv", [1, D], F32, kind="ExternalInput").ap()
    return io


def _io_consts(nc, rope=True):
    io = {}
    io["c_onesb"] = nc.dram_tensor("c_onesb", [P, 1], BF16, kind="ExternalInput").ap()
    io["c_onesrow"] = nc.dram_tensor("c_onesrow", [1, P], F32, kind="ExternalInput").ap()
    if rope:
        io["c_cos"] = nc.dram_tensor("c_cos", [P, T], BF16, kind="ExternalInput").ap()
        io["c_sin"] = nc.dram_tensor("c_sin", [P, T], BF16, kind="ExternalInput").ap()
    return io


def build_qkv(layer):
    nc = bass.Bass("TRN2", target_bir_lowering=False)
    es = ExitStack()
    io = io_qkv(nc, layer)
    io.update(_io_consts(nc))
    with es:
        pb = PB(nc, es, npb=0, nwb=0)
        emit_qkv(pb, layer, io)

        @pb.blk.sync
        def _(sy):
            pb.wait_all_dma(sy)
    return nc



def _score_ranges_l0(qt):
    jmax = min(TK0, qt * P + P)
    pieces = []
    for bank_lo in (0, 512, 1024):
        lo = bank_lo
        hi = min(bank_lo + 512, jmax)
        while lo < hi:
            if lo < META:
                seg_hi = min(hi, META)
                pieces.append((lo, seg_hi - lo, lo))
                lo = seg_hi
            else:
                pieces.append((lo, hi - lo, lo + WIN - META))
                lo = hi
    return pieces, jmax


def _score_ranges_l1(qt):
    jmax = qt * P + P
    pieces = []
    lo = 0
    while lo < jmax:
        hi = min(lo + 512, jmax)
        pieces.append((lo, hi - lo, lo))
        lo = hi
    return pieces, jmax


def _prev_key(tc, h, qtl):
    if qtl > 0:
        return (tc, h, qtl - 1)
    if h > 0:
        return (tc, h - 1, 3)
    if tc > 0:
        return (tc - 1, 15, 3)
    return None


def emit_attn(pb, layer, io):
    nh_k = KV0 if layer == 0 else 16
    vdw = TCW if layer == 0 else D
    q_d, k_d, v_d = io["q"], io["k"], io["v"]
    iden_d, tri_d, ao_d = io["iden"], io["tri"], io["ao"]

    TKmax = TK0 if layer == 0 else T
    nbank_s = (TKmax * 4 + 2047) // 2048
    rng = _score_ranges_l0 if layer == 0 else _score_ranges_l1

    if True:
        k_sb = pb.sb("k_sb", [P, nh_k * T], BF16)
        v_sb = pb.sb("v_sb", [P, NTT * vdw], BF16)
        q_sb = pb.sb("q_sb", [P, 16 * TCW], BF16)
        iden = pb.sb("iden_s", [P, P], BF16)
        tri = pb.sb("tri_s", [P, P], F32)
        Pm = pb.sb("Pm", [P, TKmax], BF16)
        P2 = pb.sb("P2", [P, TKmax], BF16)
        PT = pb.sb("PT", [P, TKmax], BF16)
        ao = pb.sb("ao_s", [P, 16 * TCW], BF16)
        rmax = pb.sb("rmax", [P, 1], F32)
        bias = pb.sb("bias", [P, 1], F32)
        rs = pb.sb("rs", [P, 1], F32)
        rdiv = pb.sb("rdiv", [P, 1], F32)

        S_ps = pb.ps("S_ps", [P, nbank_s * TCW], F32)
        PT_ps = pb.ps("PT_ps", [P, 2048], BF16)
        av_ps = pb.ps("av_ps", [P, TCW], F32)

        pe_count = pb.cnt["pe"]
        act_count = pb.cnt["act"]
        dve_count = pb.cnt["dve"]
        pe_sc_tok, pe_tr_tok, pe_av_tok = {}, {}, {}
        act_exp_tok, act_pt_tok, act_ao_tok = {}, {}, {}
        dve_mx_tok, dve_nm_tok = {}, {}
        pe_tc_done = {}
        for tc in range(NTC):
            for h in range(16):
                for qtl in range(4):
                    key = (tc, h, qtl)
                    pe_count += 1
                    pe_sc_tok[key] = pe_count
                    dve_count += 1
                    dve_mx_tok[key] = dve_count
                    act_count += 1
                    act_exp_tok[key] = act_count
                    dve_count += 1
                    dve_nm_tok[key] = dve_count
                    pe_count += 1
                    pe_tr_tok[key] = pe_count
                    act_count += 1
                    act_pt_tok[key] = act_count
                    pe_count += 1
                    pe_av_tok[key] = pe_count
                act_count += 1
                act_ao_tok[(tc, h)] = act_count
            pe_tc_done[tc] = pe_count

        @pb.blk.sync
        def _(sy):
            pb.dma(sy, iden[:, :], iden_d)
            pb.dma(sy, tri[:, :], tri_d)
            for g in range(nh_k):
                pb.dma(sy, k_sb[:, g * T:(g + 1) * T], k_d[g])
            for tt in range(NTT):
                pb.dma(sy, v_sb[:, tt * vdw:(tt + 1) * vdw], v_d[tt])
        base_loads = pb.cnt["sq"]

        q_toks = []

        @pb.blk.gpsimd
        def _(g):
            for tc in range(NTC):
                if tc >= 1:
                    g.wait_ge(pb.sem["pe"], pe_tc_done[tc - 1])
                for h in range(16):
                    tok = pb.dma(g, q_sb[:, h * TCW:(h + 1) * TCW], q_d[tc, h],
                                 sem="qld")
                q_toks.append(tok)

        @pb.blk.tensor
        def _(te):
            te.wait_ge(pb.sem["sq"], base_loads)
            for tc in range(NTC):
                te.wait_ge(pb.sem_get("qld"), q_toks[tc])
                for h in range(16):
                    g = h // 4 if layer == 0 else h
                    for qtl in range(4):
                        qt = tc * 4 + qtl
                        pieces, jmax = rng(qt)
                        nkc = (jmax + P - 1) // P
                        key = (tc, h, qtl)
                        prev = _prev_key(tc, h, qtl)
                        if prev is not None:
                            te.wait_ge(pb.sem["act"], act_exp_tok[prev])
                        lhsq = q_sb[:, h * TCW + qtl * P: h * TCW + (qtl + 1) * P]
                        for (slo, n, klo) in pieces:
                            ins = te.matmul(S_ps[:, slo:slo + n], lhsq,
                                            k_sb[:, g * T + klo: g * T + klo + n],
                                            start=True, stop=True)
                        assert pb.inc(ins, "pe") == pe_sc_tok[key]
                        te.wait_ge(pb.sem["dve"], dve_nm_tok[key])
                        if prev is not None:
                            te.wait_ge(pb.sem["act"], act_pt_tok[prev])
                        for kc in range(nkc):
                            ins = te.transpose(PT_ps[:, kc * P:(kc + 1) * P],
                                               P2[:, kc * P:(kc + 1) * P], iden[:, :])
                        assert pb.inc(ins, "pe") == pe_tr_tok[key]
                        te.wait_ge(pb.sem["act"], act_pt_tok[key])
                        if qtl == 0:
                            pk = (tc, h - 1) if h > 0 else ((tc - 1, 15) if tc > 0 else None)
                            if pk is not None:
                                te.wait_ge(pb.sem["act"], act_ao_tok[pk])
                        for kc in range(nkc):
                            vt = (0 if kc == 0 else kc + 7) if layer == 0 else kc
                            vcol = (g if layer == 0 else h) * P
                            ins = te.matmul(av_ps[:, qtl * P:(qtl + 1) * P],
                                            v_sb[:, vt * vdw + vcol: vt * vdw + vcol + P],
                                            PT[:, kc * P:(kc + 1) * P],
                                            start=(kc == 0), stop=(kc == nkc - 1))
                        assert pb.inc(ins, "pe") == pe_av_tok[key]

        @pb.blk.vector
        def _(v):
            for tc in range(NTC):
                for h in range(16):
                    for qtl in range(4):
                        qt = tc * 4 + qtl
                        pieces, jmax = rng(qt)
                        key = (tc, h, qtl)
                        prev = _prev_key(tc, h, qtl)
                        v.wait_ge(pb.sem["pe"], pe_sc_tok[key])
                        if jmax == qt * P + P:
                            v.tensor_tensor(S_ps[:, jmax - P: jmax],
                                            S_ps[:, jmax - P: jmax], tri[:, :], ALU.add)
                        pb.self_edge(v, v.reduce_max(rmax[:, 0:1],
                                                      S_ps[:, 0:jmax], AX.X))
                        ins = v.tensor_scalar_mul(bias[:, 0:1], rmax[:, 0:1], -ISQ)
                        assert pb.inc(ins, "dve") == dve_mx_tok[key]
                        v.wait_ge(pb.sem["act"], act_exp_tok[key])
                        if prev is not None:
                            v.wait_ge(pb.sem["pe"], pe_tr_tok[prev])
                        pb.self_edge(v, v.reciprocal(rdiv[:, 0:1], rs[:, 0:1]))
                        ins = v.tensor_scalar_mul(P2[:, 0:jmax], Pm[:, 0:jmax],
                                                  rdiv[:, 0:1])
                        assert pb.inc(ins, "dve") == dve_nm_tok[key]

        pb.sem_get("aos")
        aos0 = pb.cnt["aos"]

        @pb.blk.scalar
        def _(sc):
            for tc in range(NTC):
                for h in range(16):
                    for qtl in range(4):
                        qt = tc * 4 + qtl
                        pieces, jmax = rng(qt)
                        nkc = (jmax + P - 1) // P
                        key = (tc, h, qtl)
                        sc.wait_ge(pb.sem["dve"], dve_mx_tok[key])
                        ins = sc.activation(Pm[:, 0:jmax], S_ps[:, 0:jmax], AF.Exp,
                                            bias=bias[:, 0:1], scale=ISQ,
                                            accum_out=rs[:, 0:1])
                        assert pb.inc(ins, "act") == act_exp_tok[key]
                        sc.wait_ge(pb.sem["pe"], pe_tr_tok[key])
                        ins = sc.copy(PT[:, 0:nkc * P], PT_ps[:, 0:nkc * P])
                        assert pb.inc(ins, "act") == act_pt_tok[key]
                    if h == 0 and tc >= 1:
                        sc.wait_ge(pb.sem_get("aos"), aos0 + 16 * tc)
                    sc.wait_ge(pb.sem["pe"], pe_av_tok[(tc, h, 3)])
                    ins = sc.copy(ao[:, h * TCW:(h + 1) * TCW], av_ps[:, :])
                    assert pb.inc(ins, "act") == act_ao_tok[(tc, h)]

        @pb.blk.sync
        def _(sy):
            for tc in range(NTC):
                sy.wait_ge(pb.sem["act"], act_ao_tok[(tc, 15)])
                pb.dma(sy, ao_d[tc], ao[:, :], sem="aos")


def io_attn(nc, layer, pfx="", scratch=()):
    nh_k = KV0 if layer == 0 else 16
    vdw = TCW if layer == 0 else D
    kind = lambda n, out: "Internal" if n in scratch else (
        "ExternalOutput" if out else "ExternalInput")
    io = {}
    io["q"] = nc.dram_tensor(pfx + "q", [NTC, 16, P, TCW], BF16, kind=kind("q", False)).ap()
    io["k"] = nc.dram_tensor(pfx + "k", [nh_k, P, T], BF16, kind=kind("k", False)).ap()
    io["v"] = nc.dram_tensor(pfx + "v", [NTT, P, vdw], BF16, kind=kind("v", False)).ap()
    io["ao"] = nc.dram_tensor(pfx + "ao", [NTC, P, 16 * TCW], BF16, kind=kind("ao", True)).ap()
    return io


def _chunks_of(layer, qt):
    rngf = _score_ranges_l0 if layer == 0 else _score_ranges_l1
    pieces, jmax = rngf(qt)
    out = []
    lo = 0
    while lo < jmax:
        hi = min(lo + 512, jmax)
        rel = []
        for (slo, n, klo) in pieces:
            a = max(slo, lo)
            b = min(slo + n, hi)
            if a < b:
                rel.append((a - lo, b - a, klo + (a - slo)))
        out.append((lo, hi, rel, hi == jmax and jmax == qt * P + P))
        lo = hi
    return out, jmax


def emit_attn2(pb, layer, io):
    nh_k = KV0 if layer == 0 else 16
    vdw = TCW if layer == 0 else D
    q_d, k_d, v_d = io["q"], io["k"], io["v"]
    iden_d, tri_d, ao_d = io["iden"], io["tri"], io["ao"]
    TKmax = TK0 if layer == 0 else T
    DP = 2

    k_sb = pb.sb("k_sb", [P, nh_k * T], BF16)
    v_sb = pb.sb("v_sb", [P, NTT * vdw], BF16)
    q_sb = pb.sb("q_sb", [P, 16 * TCW], BF16)
    iden = pb.sb("iden_s", [P, P], BF16)
    tri = pb.sb("tri_s", [P, P], BF16)
    Pm = [pb.sb(f"Pm{i}", [P, TKmax], BF16) for i in range(2)]
    P2 = [pb.sb(f"P2{i}", [P, TKmax], BF16) for i in range(2)]
    PT = pb.sb("PT", [P, TKmax], BF16)
    ao = pb.sb("ao_s", [P, 16 * TCW], BF16)
    nm = [pb.sb(f"nm{i}", [P, 4], F32) for i in range(2)]
    sa = [pb.sb(f"sa{i}", [P, 4], F32) for i in range(2)]
    fa = [pb.sb(f"fa{i}", [P, 4], F32) for i in range(2)]
    sc_all = [pb.sb(f"sc{i}", [P, 4], F32) for i in range(2)]
    ngm = [pb.sb(f"ngm{i}", [P, 1], F32) for i in range(2)]
    stot = pb.sb("stot", [P, 1], F32)
    rdiv = pb.sb("rdiv", [P, 1], F32)
    ssc = pb.sb("ssc", [P, 4], F32)

    S_ps = [pb.ps(f"S_ps{i}", [P, TCW], F32) for i in range(4)]
    PT_ps = pb.ps("PT_ps", [P, 2048], BF16)
    av_ps = pb.ps("av_ps", [P, TCW], F32)

    blocks = [(tc, h, qtl) for tc in range(NTC) for h in range(16)
              for qtl in range(4)]
    nblk = len(blocks)
    binfo = []
    for (tc, h, qtl) in blocks:
        ch, jmax = _chunks_of(layer, tc * 4 + qtl)
        binfo.append((ch, jmax))

    pe0 = pb.cnt["pe"]
    act0 = pb.cnt["act"]
    dve0 = pb.cnt["dve"]
    pe_sc = {}
    pe_tr = {}
    pe_av = {}
    act_exp = {}
    act_f = {}
    act_pt = {}
    act_ao = {}
    dve_mx = {}
    dve_gm = {}
    dve_rs = {}
    pe_c = pe0
    act_c = act0
    dve_c = dve0
    cg_of = {}
    cg = 0
    pe_tc_done = {}
    for i in range(nblk + DP):
        if i < nblk:
            for kb in range(len(binfo[i][0])):
                pe_c += 1
                pe_sc[(i, kb)] = pe_c
                cg_of[(i, kb)] = cg
                cg += 1
            tc, h, qtl = blocks[i]
            if (h, qtl) == (15, 3):
                pe_tc_done[tc] = pe_c
        j = i - DP
        if j >= 0:
            pe_c += 1
            pe_tr[j] = pe_c
            pe_c += 1
            pe_av[j] = pe_c
    for i in range(nblk + 1):
        if i < nblk:
            for kb in range(len(binfo[i][0])):
                dve_c += 1
                dve_mx[(i, kb)] = dve_c
            dve_c += 1
            dve_gm[i] = dve_c
        if i >= 1:
            dve_c += 3
            dve_rs[i - 1] = dve_c
    for i in range(nblk):
        for kb in range(len(binfo[i][0])):
            act_c += 1
            act_exp[(i, kb)] = act_c
        act_c += 1
        act_f[i] = act_c
        j = i - DP
        if j >= 0:
            act_c += 1
            act_pt[j] = act_c
            tc, h, qtl = blocks[j]
            if qtl == 3:
                act_c += 1
                act_ao[(tc, h)] = act_c
    for j in range(nblk - DP, nblk):
        act_c += 1
        act_pt[j] = act_c
        tc, h, qtl = blocks[j]
        if qtl == 3:
            act_c += 1
            act_ao[(tc, h)] = act_c

    cg_inv = {v: k for k, v in cg_of.items()}

    @pb.blk.sync
    def _(sy):
        pb.dma(sy, iden[:, :], iden_d)
        pb.dma(sy, tri[:, :], tri_d)
        for g in range(nh_k):
            pb.dma(sy, k_sb[:, g * T:(g + 1) * T], k_d[g])
        for tt in range(NTT):
            pb.dma(sy, v_sb[:, tt * vdw:(tt + 1) * vdw], v_d[tt])
    base_loads = pb.cnt["sq"]

    q_toks = []

    @pb.blk.gpsimd
    def _(g):
        for tc in range(NTC):
            if tc >= 1:
                g.wait_ge(pb.sem["pe"], pe_tc_done[tc - 1])
            for h in range(16):
                tok = pb.dma(g, q_sb[:, h * TCW:(h + 1) * TCW], q_d[tc, h],
                             sem="qld")
            q_toks.append(tok)

    @pb.blk.tensor
    def _(te):
        te.wait_ge(pb.sem["sq"], base_loads)
        for i in range(nblk + DP):
            if i < nblk:
                tc, h, qtl = blocks[i]
                g = h // 4 if layer == 0 else h
                chs, jmax = binfo[i]
                if h == 0 and qtl == 0:
                    te.wait_ge(pb.sem_get("qld"), q_toks[tc])
                lhsq = q_sb[:, h * TCW + qtl * P: h * TCW + (qtl + 1) * P]
                for kb, (lo, hi, rel, hasdiag) in enumerate(chs):
                    mycg = cg_of[(i, kb)]
                    if mycg >= 4:
                        pi, pkb = cg_inv[mycg - 4]
                        te.wait_ge(pb.sem["act"], act_exp[(pi, pkb)])
                    bank = S_ps[mycg % 4]
                    for (rlo, n, klo) in rel:
                        ins = te.matmul(bank[:, rlo:rlo + n], lhsq,
                                        k_sb[:, g * T + klo: g * T + klo + n],
                                        start=True, stop=True)
                    if hasdiag:
                        w = hi - lo
                        ins = te.matmul(bank[:, w - P:w], iden[:, :], tri[:, :],
                                        start=False, stop=True)
                    assert pb.inc(ins, "pe") == pe_sc[(i, kb)]
            j = i - DP
            if j >= 0:
                tc, h, qtl = blocks[j]
                g = h // 4 if layer == 0 else h
                chs, jmax = binfo[j]
                nkc = jmax // P
                te.wait_ge(pb.sem["dve"], dve_rs[j])
                if j >= 1:
                    te.wait_ge(pb.sem["act"], act_pt[j - 1])
                for kc in range(nkc):
                    ins = te.transpose(PT_ps[:, kc * P:(kc + 1) * P],
                                       P2[j % 2][:, kc * P:(kc + 1) * P], iden[:, :])
                assert pb.inc(ins, "pe") == pe_tr[j]
                te.wait_ge(pb.sem["act"], act_pt[j])
                if qtl == 0:
                    pk = (tc, h - 1) if h > 0 else ((tc - 1, 15) if tc > 0 else None)
                    if pk is not None:
                        te.wait_ge(pb.sem["act"], act_ao[pk])
                for kc in range(nkc):
                    vt = (0 if kc == 0 else kc + 7) if layer == 0 else kc
                    vcol = (g if layer == 0 else h) * P
                    ins = te.matmul(av_ps[:, qtl * P:(qtl + 1) * P],
                                    v_sb[:, vt * vdw + vcol: vt * vdw + vcol + P],
                                    PT[:, kc * P:(kc + 1) * P],
                                    start=(kc == 0), stop=(kc == nkc - 1))
                assert pb.inc(ins, "pe") == pe_av[j]

    def _combine_rest(v, i):
        chs, jmax = binfo[i]
        KB = len(chs)
        b2 = i % 2
        if KB > 1:
            v.wait_ge(pb.sem["act"], act_f[i])
            ins = v.scalar_tensor_tensor(ssc[:, 0:KB], fa[b2][:, 0:KB], 1.0,
                                         sa[b2][:, 0:KB], ALU.mult, ALU.mult,
                                         accum_out=stot[:, 0:1])
            v.wait_ge(pb.sem["dve"], pb.inc(ins, "dve"))
            ins = v.reciprocal(rdiv[:, 0:1], stot[:, 0:1])
        else:
            v.wait_ge(pb.sem["act"], act_exp[(i, 0)])
            ins = v.reciprocal(rdiv[:, 0:1], sa[b2][:, 0:1])
            pb.inc(ins, "dve")
            v.wait_ge(pb.sem["dve"], pb.cnt["dve"])
            ins = v.tensor_copy(ssc[:, 0:1], rdiv[:, 0:1])
        v.wait_ge(pb.sem["dve"], pb.inc(ins, "dve"))
        if i >= 2:
            v.wait_ge(pb.sem["pe"], pe_tr[i - 2])
        for kb, (lo, hi, rel, hasdiag) in enumerate(chs):
            if KB > 1:
                ins = v.tensor_scalar(P2[b2][:, lo:hi], Pm[b2][:, lo:hi],
                                      fa[b2][:, kb:kb + 1], rdiv[:, 0:1],
                                      ALU.mult, ALU.mult)
            else:
                ins = v.tensor_scalar_mul(P2[b2][:, lo:hi], Pm[b2][:, lo:hi],
                                          rdiv[:, 0:1])
        assert pb.inc(ins, "dve") == dve_rs[i]

    @pb.blk.vector
    def _(v):
        for i in range(nblk + 1):
            if i < nblk:
                chs, jmax = binfo[i]
                KB = len(chs)
                b2 = i % 2
                if i >= 2:
                    v.wait_ge(pb.sem["act"], act_f[i - 2])
                for kb, (lo, hi, rel, hasdiag) in enumerate(chs):
                    v.wait_ge(pb.sem["pe"], pe_sc[(i, kb)])
                    bank = S_ps[cg_of[(i, kb)] % 4]
                    w = hi - lo
                    ins = v.reduce_max(nm[b2][:, kb:kb + 1], bank[:, 0:w], AX.X,
                                       negate=True)
                    assert pb.inc(ins, "dve") == dve_mx[(i, kb)]
                v.wait_ge(pb.sem["dve"], dve_mx[(i, KB - 1)])
                ins = v.tensor_reduce(ngm[b2][:, 0:1], nm[b2][:, 0:KB], AX.X,
                                      ALU.min)
                assert pb.inc(ins, "dve") == dve_gm[i]
            if i >= 1:
                _combine_rest(v, i - 1)

    pb.sem_get("aos")
    aos0 = pb.cnt["aos"]

    @pb.blk.scalar
    def _(sc):
        for i in range(nblk):
            chs, jmax = binfo[i]
            KB = len(chs)
            b2 = i % 2
            tc, h, qtl = blocks[i]
            if i >= 2:
                sc.wait_ge(pb.sem["dve"], dve_rs[i - 2])
            for kb, (lo, hi, rel, hasdiag) in enumerate(chs):
                sc.wait_ge(pb.sem["dve"], dve_mx[(i, kb)])
                bank = S_ps[cg_of[(i, kb)] % 4]
                ins = sc.activation(Pm[b2][:, lo:hi], bank[:, 0:hi - lo], AF.Exp,
                                    bias=nm[b2][:, kb:kb + 1],
                                    accum_out=sa[b2][:, kb:kb + 1])
                assert pb.inc(ins, "act") == act_exp[(i, kb)]
            sc.wait_ge(pb.sem["dve"], dve_gm[i])
            ins = sc.activation(fa[b2][:, 0:KB], nm[b2][:, 0:KB], AF.Exp,
                                bias=ngm[b2][:, 0:1], scale=-1.0)
            assert pb.inc(ins, "act") == act_f[i]
            j = i - DP
            if j >= 0:
                _emit_ptao(pb, sc, j, blocks, binfo, PT, PT_ps, ao, av_ps,
                           pe_tr, pe_av, act_pt, act_ao, aos0, base_loads)
        for j in range(nblk - DP, nblk):
            _emit_ptao(pb, sc, j, blocks, binfo, PT, PT_ps, ao, av_ps,
                       pe_tr, pe_av, act_pt, act_ao, aos0, base_loads)

    @pb.blk.sync
    def _(sy):
        for tc in range(NTC):
            sy.wait_ge(pb.sem["act"], act_ao[(tc, 15)])
            pb.dma(sy, ao_d[tc], ao[:, :], sem="aos")


def _emit_ptao(pb, sc, j, blocks, binfo, PT, PT_ps, ao, av_ps,
               pe_tr, pe_av, act_pt, act_ao, aos0, base_loads):
    tc, h, qtl = blocks[j]
    chs, jmax = binfo[j]
    nkc = jmax // P
    sc.wait_ge(pb.sem["pe"], pe_tr[j])
    ins = sc.copy(PT[:, 0:nkc * P], PT_ps[:, 0:nkc * P])
    assert pb.inc(ins, "act") == act_pt[j]
    if qtl == 3:
        if h == 0 and tc >= 1:
            sc.wait_ge(pb.sem_get("aos"), aos0 + 16 * tc)
        sc.wait_ge(pb.sem["pe"], pe_av[j])
        ins = sc.copy(ao[:, h * TCW:(h + 1) * TCW], av_ps[:, :])
        assert pb.inc(ins, "act") == act_ao[(tc, h)]


def build_attncore(layer):
    nc = bass.Bass("TRN2", target_bir_lowering=False)
    es = ExitStack()
    io = io_attn(nc, layer)
    io["iden"] = nc.dram_tensor("iden", [P, P], BF16, kind="ExternalInput").ap()
    io["tri"] = nc.dram_tensor("tri", [P, P], BF16, kind="ExternalInput").ap()
    with es:
        pb = PB(nc, es, npb=0, nwb=0)
        emit_attn2(pb, layer, io)

        @pb.blk.sync
        def _(sy):
            pb.wait_all_dma(sy)
    return nc



def emit_wo(pb, layer, io):
    ao_d, wo_d, x_d, xo_d = io["ao"], io["wo"], io["x"], io["xo"]
    if layer == 1:
        bo_d = io["bo"]
    if True:
        pb.alloc_pool(4, 4)
        ao = [pb.sb(f"ao_s{i}", [P, 16 * TCW], BF16) for i in range(2)]
        xt = [pb.sb(f"xt{i}", [P, TCW], F32) for i in range(4)]
        xo = [pb.sb(f"xo{i}", [P, TCW], F32) for i in range(2)]
        if layer == 1:
            bo_sb = pb.sb("bo_s", [P, 16], F32)

            @pb.blk.sync
            def _(sy):
                pb.dma(sy, bo_sb[:, :], bo_d)
            pb.barrier()

        gq_toks = []

        @pb.blk.gpsimd
        def _(g):
            for sl in range(4):
                gq_toks.append(pb.dma(g, pb.wbuf[sl][:, :], wo_d[sl], sem=f"w{sl}"))

        pe_g0 = pb.cnt["pe"]
        dve_g0 = pb.cnt["dve"]
        ao_toks = []

        @pb.blk.gpsimd
        def _(g):
            for tc in range(NTC):
                pb.wait(g, "pe", pe_g0 + 16 * (tc - 1))
                ao_toks.append(pb.dma(g, ao[tc % 2][:, :], ao_d[tc],
                                      sem=f"aold{tc % 2}"))

        @pb.blk.tensor
        def _(te):
            for sl in range(4):
                te.wait_ge(pb.sem_get(f"w{sl}"), gq_toks[sl])
            for tc in range(NTC):
                te.wait_ge(pb.sem_get(f"aold{tc % 2}"), ao_toks[tc])
                for gi in range(16):
                    sl, osub = gi // 4, gi % 4
                    G = tc * 16 + gi
                    bank = pb.pbank[G % 4]
                    if G >= 4:
                        te.wait_ge(pb.sem["dve"], dve_g0 + G - 3)
                    for c in range(DC):
                        ins = te.matmul(bank[:, :],
                                        pb.wbuf[sl][:, c * TCW + osub * P:
                                                    c * TCW + osub * P + P],
                                        ao[tc % 2][:, c * TCW:(c + 1) * TCW],
                                        start=(c == 0), stop=(c == DC - 1))
                    pb.inc(ins, "pe")

        if layer == 1:
            def dve_op(v, g, xo_t, bank, xt_t):
                return v.scalar_tensor_tensor(xo_t[:, :], bank[:, :],
                                              bo_sb[:, g % 16:g % 16 + 1], xt_t[:, :],
                                              ALU.add, ALU.add)
        else:
            def dve_op(v, g, xo_t, bank, xt_t):
                return v.tensor_tensor(xo_t[:, :], bank[:, :], xt_t[:, :], ALU.add)

        emit_resid_stream(pb, 4 * 16, pe_g0, lambda g: pb.pbank[g % 4],
                          lambda g: x_d[g % 16][:, (g // 16) * TCW:(g // 16 + 1) * TCW],
                          lambda g: xo_d[g % 16][:, (g // 16) * TCW:(g // 16 + 1) * TCW],
                          dve_op, xt, xo)


def build_woresid(layer):
    nc = bass.Bass("TRN2", target_bir_lowering=False)
    es = ExitStack()
    io = {}
    io["ao"] = nc.dram_tensor("ao", [NTC, P, 16 * TCW], BF16, kind="ExternalInput").ap()
    io["wo"] = nc.dram_tensor("wo", [4, P, DC * TCW], F32, kind="ExternalInput").ap()
    io["x"] = nc.dram_tensor("x", [DC, P, T], F32, kind="ExternalInput").ap()
    io["xo"] = nc.dram_tensor("xo", [DC, P, T], F32, kind="ExternalOutput").ap()
    if layer == 1:
        io["bo"] = nc.dram_tensor("bo", [P, 16], F32, kind="ExternalInput").ap()
    with es:
        pb = PB(nc, es, npb=0, nwb=0)
        emit_wo(pb, layer, io)

        @pb.blk.sync
        def _(sy):
            pb.wait_all_dma(sy)
    return nc



def emit_ffn(pb, layer, io):
    x_d, xo_d = io["x"], io["xo"]
    nic = IC0 if layer == 0 else IC1
    ngs = nic // 4
    nds = DC
    wg_d, wu_d, wd_d = io["wg"], io["wu"], io["wd"]
    h_scr = io["hscr"]
    onesb_d, onesrow_d = io["c_onesb"], io["c_onesrow"]
    if layer == 1:
        gs_d, us_d, ds_d = io["gs"], io["us"], io["ds"]
    if True:
        pb.alloc_pool(4, 4)
        lnb = mk_ln_bufs(pb, ping=False)
        xn = pb.sb("xn", [P, DC * T], BF16)
        sg = pb.sb("sg", [P, TCW], BF16)
        hq = [pb.sb(f"hq{i}", [P, 4 * TCW], BF16) for i in range(2)]
        xt = [pb.sb(f"xt{i}", [P, TCW], F32) for i in range(2)]
        xo = [pb.sb(f"xo{i}", [P, TCW], F32) for i in range(2)]
        if layer == 0:
            mt = [pb.sb(f"m24_{i}", [P, TCW], BF16) for i in range(10)]
        if layer == 1:
            gs_sb = pb.sb("gs_s", [P, nic], F32)
            us_sb = pb.sb("us_s", [P, nic], F32)
            ds_sb = pb.sb("ds_s", [P, DC], F32)

        @pb.blk.sync
        def _(sy):
            pb.dma(sy, lnb["onesb"][:, 0:1], onesb_d)
            pb.dma(sy, lnb["ones_row"][0:1, :], onesrow_d)
            if layer == 1:
                pb.dma(sy, gs_sb[:, :], gs_d)
                pb.dma(sy, us_sb[:, :], us_d)
                pb.dma(sy, ds_sb[:, :], ds_d)
        pb.barrier()

        emit_ln(pb, x_d, lnb, list(range(NTC)),
                lambda c, tc: xn[:, c * T + tc * TCW: c * T + (tc + 1) * TCW])
        pb.barrier()

        pe0 = pb.cnt["pe"]
        act0 = pb.cnt["act"]
        dve0 = pb.cnt["dve"]
        pb.sem_get("hst0")
        pb.sem_get("hst1")
        hst0p = [pb.cnt["hst0"], pb.cnt["hst1"]]
        gq_toks = []
        nst = ngs * NTC
        GPI = 5 if layer == 0 else 4

        def evac_tok(j):
            return dve0 + (j // 4) * GPI + (j % 4) + 1

        @pb.blk.gpsimd
        def _(g):
            for sl in range(ngs):
                if sl >= 2:
                    g.wait_ge(pb.sem["pe"], pe0 + 32 * (sl - 1))
                t1 = pb.dma(g, pb.wbuf[(2 * sl) % 4][:, :], wg_d[sl],
                            sem=f"w{(2 * sl) % 4}")
                t2 = pb.dma(g, pb.wbuf[(2 * sl + 1) % 4][:, :], wu_d[sl],
                            sem=f"w{(2 * sl + 1) % 4}")
                gq_toks.append((t1, t2))

        @pb.blk.tensor
        def _(te):
            for sl in range(ngs):
                te.wait_ge(pb.sem_get(f"w{(2 * sl) % 4}"), gq_toks[sl][0])
                te.wait_ge(pb.sem_get(f"w{(2 * sl + 1) % 4}"), gq_toks[sl][1])
                for tc in range(NTC):
                    for osub in range(4):
                        k = (sl * NTC + tc) * 4 + osub
                        gb = pb.pbank[(2 * k) % 4]
                        ub = pb.pbank[(2 * k + 1) % 4]
                        pb.wait(te, "act", act0 + k - 1)
                        if k >= 2:
                            te.wait_ge(pb.sem["dve"], evac_tok(k - 2))
                        for c in range(DC):
                            ins = te.matmul(gb[:, :],
                                            pb.wbuf[(2 * sl) % 4][:, c * TCW + osub * P:
                                                                  c * TCW + osub * P + P],
                                            xn[:, c * T + tc * TCW: c * T + (tc + 1) * TCW],
                                            start=(c == 0), stop=(c == DC - 1))
                        pb.inc(ins, "pe")
                        for c in range(DC):
                            ins = te.matmul(ub[:, :],
                                            pb.wbuf[(2 * sl + 1) % 4][:, c * TCW + osub * P:
                                                                      c * TCW + osub * P + P],
                                            xn[:, c * T + tc * TCW: c * T + (tc + 1) * TCW],
                                            start=(c == 0), stop=(c == DC - 1))
                        pb.inc(ins, "pe")

        @pb.blk.scalar
        def _(sc):
            for k in range(4 * nst):
                sl = k // 16
                sc.wait_ge(pb.sem["pe"], pe0 + 2 * k + 1)
                if k >= 1:
                    sc.wait_ge(pb.sem["dve"], evac_tok(k - 1))
                gb = pb.pbank[(2 * k) % 4]
                if layer == 1:
                    ic = (k // 4) // NTC * 4 + k % 4
                    ins = sc.activation(sg[:, :], gb[:, :], AF.Silu,
                                        scale=gs_sb[:, ic:ic + 1])
                else:
                    ins = sc.activation(sg[:, :], gb[:, :], AF.Silu)
                pb.inc(ins, "act")

        @pb.blk.vector
        def _(v):
            st_idx = 0
            for st in range(nst):
                sl, tc = st // NTC, st % NTC
                buf = hq[st % 2]
                if st >= 2:
                    v.wait_ge(pb.sem[f"hst{st % 2}"],
                              hst0p[st % 2] + 16 * ((st - 2) // 2 + 1))
                for osub in range(4):
                    k = st * 4 + osub
                    ic = sl * 4 + osub
                    v.wait_ge(pb.sem["pe"], pe0 + 2 * k + 2)
                    v.wait_ge(pb.sem["act"], act0 + k + 1)
                    ub = pb.pbank[(2 * k + 1) % 4]
                    dst = buf[:, osub * TCW:(osub + 1) * TCW]
                    if layer == 1:
                        ins = v.scalar_tensor_tensor(dst, ub[:, :], us_sb[:, ic:ic + 1],
                                                     sg[:, :], ALU.mult, ALU.mult)
                    else:
                        ins = v.tensor_tensor(dst, ub[:, :], sg[:, :], ALU.mult)
                    pb.inc(ins, "dve")
                if layer == 0:
                    a0, a1, a2, a3 = mt[0:4]
                    c01, c02, c03, c12, c13, c23 = mt[4:10]
                    A = [buf[:, kk * TCW:(kk + 1) * TCW] for kk in range(4)]
                    for kk, dstt in enumerate((a0, a1, a2, a3)):
                        v.tensor_single_scalar(dstt[:, :].bitcast(mybir.dt.uint16),
                                               A[kk].bitcast(mybir.dt.uint16),
                                               0x7FFF, ALU.bitwise_and)
                    v.tensor_tensor(c01[:, :], a0[:, :], a1[:, :], ALU.is_ge)
                    v.tensor_tensor(c02[:, :], a0[:, :], a2[:, :], ALU.is_ge)
                    v.tensor_tensor(c03[:, :], a0[:, :], a3[:, :], ALU.is_ge)
                    v.tensor_tensor(c12[:, :], a1[:, :], a2[:, :], ALU.is_ge)
                    v.tensor_tensor(c13[:, :], a1[:, :], a3[:, :], ALU.is_ge)
                    v.tensor_tensor(c23[:, :], a2[:, :], a3[:, :], ALU.is_ge)
                    v.tensor_tensor(a0[:, :], c01[:, :], c02[:, :], ALU.add)
                    v.tensor_tensor(a0[:, :], a0[:, :], c03[:, :], ALU.add)
                    v.tensor_single_scalar(a0[:, :], a0[:, :], 2.0, ALU.is_ge)
                    v.tensor_tensor(a1[:, :], c12[:, :], c13[:, :], ALU.add)
                    v.tensor_tensor(a1[:, :], a1[:, :], c01[:, :], ALU.subtract)
                    v.tensor_single_scalar(a1[:, :], a1[:, :], 1.0, ALU.is_ge)
                    v.tensor_tensor(a2[:, :], c02[:, :], c12[:, :], ALU.add)
                    v.tensor_tensor(a2[:, :], a2[:, :], c23[:, :], ALU.subtract)
                    v.tensor_single_scalar(a2[:, :], a2[:, :], 0.0, ALU.is_le)
                    v.tensor_tensor(a3[:, :], c03[:, :], c13[:, :], ALU.add)
                    v.tensor_tensor(a3[:, :], a3[:, :], c23[:, :], ALU.add)
                    v.tensor_single_scalar(a3[:, :], a3[:, :], 1.0, ALU.is_le)
                    for kk, msk in enumerate((a0, a1, a2, a3)):
                        ins = v.tensor_tensor(A[kk], A[kk], msk[:, :], ALU.mult)
                    pb.inc(ins, "dve")

        @pb.blk.sync
        def _(sy):
            for st in range(nst):
                sl, tc = st // NTC, st % NTC
                sy.wait_ge(pb.sem["dve"], dve0 + (st + 1) * GPI)
                pb.dma(sy, h_scr[sl, tc], hq[st % 2][:, :], sem=f"hst{st % 2}")

        pb.barrier()

        peC0 = pb.cnt["pe"]
        dveC0 = pb.cnt["dve"]
        for tc in range(NTC):
            pb.sem_get("hld")

            @pb.blk.sync
            def _(sy):
                pb.wait(sy, "pe", peC0 + 16 * tc)
                for sl in range(ngs):
                    pb.dma(sy, xn[:, sl * 4 * TCW:(sl + 1) * 4 * TCW],
                           h_scr[sl, tc], sem="hld")
            hld_tok = pb.cnt["hld"]

            gqd_toks = []

            @pb.blk.gpsimd
            def _(g):
                for sl in range(nds):
                    Gg = tc * nds + sl
                    if Gg >= 4:
                        g.wait_ge(pb.sem["pe"], peC0 + (Gg - 3))
                    gqd_toks.append(pb.dma(g, pb.wbuf[sl % 4][:, 0:nic * P], wd_d[sl],
                                           sem=f"w{sl % 4}"))

            @pb.blk.tensor
            def _(te):
                te.wait_ge(pb.sem["hld"], hld_tok)
                for sl in range(nds):
                    Gg = tc * nds + sl
                    te.wait_ge(pb.sem_get(f"w{sl % 4}"), gqd_toks[sl])
                    bank = pb.pbank[Gg % 4]
                    if Gg >= 4:
                        te.wait_ge(pb.sem["dve"], dveC0 + Gg - 3)
                    for c in range(nic):
                        ins = te.matmul(bank[:, :],
                                        pb.wbuf[sl % 4][:, c * P: (c + 1) * P],
                                        xn[:, c * TCW:(c + 1) * TCW],
                                        start=(c == 0), stop=(c == nic - 1))
                    pb.inc(ins, "pe")

            if layer == 1:
                def dve_op(v, g, xo_t, bank, xt_t):
                    return v.scalar_tensor_tensor(xo_t[:, :], bank[:, :],
                                                  ds_sb[:, g:g + 1], xt_t[:, :],
                                                  ALU.mult, ALU.add)
            else:
                def dve_op(v, g, xo_t, bank, xt_t):
                    return v.tensor_tensor(xo_t[:, :], bank[:, :], xt_t[:, :], ALU.add)

            emit_resid_stream(pb, nds, peC0 + 16 * tc, lambda g: pb.pbank[(tc * nds + g) % 4],
                              lambda g: x_d[g][:, tc * TCW:(tc + 1) * TCW],
                              lambda g: xo_d[g][:, tc * TCW:(tc + 1) * TCW],
                              dve_op, xt, xo)


def io_ffn(nc, layer, pfx="", scratch=()):
    nic = IC0 if layer == 0 else IC1
    ngs = nic // 4
    kind = lambda n, out: "Internal" if n in scratch else (
        "ExternalOutput" if out else "ExternalInput")
    io = {}
    io["x"] = nc.dram_tensor(pfx + "x", [DC, P, T], F32, kind=kind("x", False)).ap()
    io["xo"] = nc.dram_tensor(pfx + "xo", [DC, P, T], F32, kind=kind("xo", True)).ap()
    io["wg"] = nc.dram_tensor(pfx + "wg", [ngs, P, DC * TCW], F32, kind="ExternalInput").ap()
    io["wu"] = nc.dram_tensor(pfx + "wu", [ngs, P, DC * TCW], F32, kind="ExternalInput").ap()
    io["wd"] = nc.dram_tensor(pfx + "wd", [DC, P, nic * P], F32, kind="ExternalInput").ap()
    io["hscr"] = nc.dram_tensor(pfx + "hscr", [ngs, NTC, P, 4 * TCW], BF16,
                                kind="Internal").ap()
    if layer == 1:
        io["gs"] = nc.dram_tensor(pfx + "gs", [P, nic], F32, kind="ExternalInput").ap()
        io["us"] = nc.dram_tensor(pfx + "us", [P, nic], F32, kind="ExternalInput").ap()
        io["ds"] = nc.dram_tensor(pfx + "ds", [P, DC], F32, kind="ExternalInput").ap()
    return io


def build_ffn(layer):
    nc = bass.Bass("TRN2", target_bir_lowering=False)
    es = ExitStack()
    io = io_ffn(nc, layer)
    io.update(_io_consts(nc, rope=False))
    with es:
        pb = PB(nc, es, npb=0, nwb=0)
        emit_ffn(pb, layer, io)

        @pb.blk.sync
        def _(sy):
            pb.wait_all_dma(sy)
    return nc



def build_fused():
    nc = bass.Bass("TRN2", target_bir_lowering=False)
    es = ExitStack()

    consts = _io_consts(nc, rope=True)
    consts["iden"] = nc.dram_tensor("iden", [P, P], BF16, kind="ExternalInput").ap()
    consts["tri"] = nc.dram_tensor("tri", [P, P], BF16, kind="ExternalInput").ap()

    x_in = nc.dram_tensor("x", [DC, P, T], F32, kind="ExternalInput").ap()
    x_out = nc.dram_tensor("xo", [DC, P, T], F32, kind="ExternalOutput").ap()
    x1 = nc.dram_tensor("x1", [DC, P, T], F32, kind="Internal").ap()
    x2 = nc.dram_tensor("x2", [DC, P, T], F32, kind="Internal").ap()
    x3 = nc.dram_tensor("x3", [DC, P, T], F32, kind="Internal").ap()

    ioq0 = io_qkv(nc, 0, pfx="a0_", scratch=("x", "q", "k", "v"))
    ioq0["x"] = x_in
    ioq0.update(consts)
    ioa0 = {k: ioq0[k] for k in ("q", "k", "v")}
    ioa0["ao"] = nc.dram_tensor("a0_ao", [NTC, P, 16 * TCW], BF16, kind="Internal").ap()
    ioa0.update(consts)
    iow0 = {"ao": ioa0["ao"], "x": x_in, "xo": x1,
            "wo": nc.dram_tensor("a0_wo", [4, P, DC * TCW], F32, kind="ExternalInput").ap()}
    iof0 = io_ffn(nc, 0, pfx="f0_", scratch=("x", "xo"))
    iof0["x"] = x1
    iof0["xo"] = x2
    iof0.update(consts)

    ioq1 = io_qkv(nc, 1, pfx="a1_", scratch=("x", "q", "k", "v"))
    ioq1["x"] = x2
    ioq1.update(consts)
    ioa1 = {k: ioq1[k] for k in ("q", "k", "v")}
    ioa1["ao"] = nc.dram_tensor("a1_ao", [NTC, P, 16 * TCW], BF16, kind="Internal").ap()
    ioa1.update(consts)
    iow1 = {"ao": ioa1["ao"], "x": x2, "xo": x3,
            "wo": nc.dram_tensor("a1_wo", [4, P, DC * TCW], F32, kind="ExternalInput").ap(),
            "bo": nc.dram_tensor("a1_bo", [P, 16], F32, kind="ExternalInput").ap()}
    iof1 = io_ffn(nc, 1, pfx="f1_", scratch=("x", "xo"))
    iof1["x"] = x3
    iof1["xo"] = x_out
    iof1.update(consts)

    phases = [("q0_", emit_qkv, 0, ioq0), ("t0_", emit_attn2, 0, ioa0),
              ("w0_", emit_wo, 0, iow0), ("m0_", emit_ffn, 0, iof0),
              ("q1_", emit_qkv, 1, ioq1), ("t1_", emit_attn2, 1, ioa1),
              ("w1_", emit_wo, 1, iow1), ("m1_", emit_ffn, 1, iof1)]

    with es:
        pb = PB(nc, es, npb=0, nwb=0)
        for pfx, emit, layer, io in phases:
            pb.push(pfx)
            with nc.named_scope(pfx):
                emit(pb, layer, io)
                pb.barrier()
            pb.pop()

        @pb.blk.sync
        def _(sy):
            pb.wait_all_dma(sy)
    return nc



_CACHE = {}


def _consts_host():
    cos, sin = rope_tables()
    return {
        "c_onesb": np.ones((P, 1), dtype=BF),
        "c_onesrow": np.ones((1, P), dtype=np.float32),
        "c_cos": (np.concatenate([cos, cos], axis=0) * math.sqrt(ISQ)).astype(BF),
        "c_sin": (np.concatenate([sin, sin], axis=0) * math.sqrt(ISQ)).astype(BF),
        "iden": np.eye(P, dtype=BF),
        "tri": np.triu(np.full((P, P), NEG, dtype=np.float32), 1).astype(BF),
    }


def _cget(key, fn):
    if key not in _CACHE:
        _CACHE[key] = fn()
    return _CACHE[key]


def _kernel_unfused(**inp):
    ch = _consts_host()
    rp = _rope_perm()
    p24 = _perm24()

    wq0 = prep_w(col_perm_heads(np.asarray(inp["L0_attn_Wq"], np.float32), rp), TCW)
    wk0 = prep_w(col_perm_heads(np.asarray(inp["L0_attn_Wk"], np.float32), rp), TCW)
    wv0 = prep_w(np.asarray(inp["L0_attn_Wv"], np.float32), TCW)[0]
    wo0 = prep_w(np.asarray(inp["L0_attn_Wo"], np.float32), TCW)
    wg0 = prep_w(np.asarray(inp["L0_ffn_w_gate"], np.float32)[:, p24], TCW)
    wu0 = prep_w(np.asarray(inp["L0_ffn_w_up"], np.float32)[:, p24], TCW)
    wd0 = prep_w(np.asarray(inp["L0_ffn_w_down"], np.float32)[p24, :], P)

    wq1 = prep_w(col_perm_heads(np.asarray(inp["L1_attn_Wq"], np.float32).T, rp), TCW)
    wk1 = prep_w(col_perm_heads(np.asarray(inp["L1_attn_Wk"], np.float32).T, rp), TCW)
    wv1 = prep_w(np.ascontiguousarray(np.asarray(inp["L1_attn_Wv"], np.float32).T), TCW)
    wo1 = prep_w(np.ascontiguousarray(np.asarray(inp["L1_attn_Wo"], np.float32).T), TCW)
    bq1 = bias_cols(col_perm_heads(np.asarray(inp["L1_attn_bq"], np.float32)[None, :], rp)[0])
    bk1 = bias_cols(col_perm_heads(np.asarray(inp["L1_attn_bk"], np.float32)[None, :], rp)[0])
    bv1 = np.ascontiguousarray(np.asarray(inp["L1_attn_bv"], np.float32)[None, :])
    bo1 = bias_cols(np.asarray(inp["L1_attn_bo"], np.float32))
    wg1 = prep_w(np.ascontiguousarray(np.asarray(inp["L1_ffn_gate_w"], np.float32).T), TCW)
    wu1 = prep_w(np.ascontiguousarray(np.asarray(inp["L1_ffn_up_w"], np.float32).T), TCW)
    wd1 = prep_w(np.ascontiguousarray(np.asarray(inp["L1_ffn_down_w"], np.float32).T), P)
    gs1 = bias_cols(np.asarray(inp["L1_ffn_gate_s"], np.float32))
    us1 = bias_cols(np.asarray(inp["L1_ffn_up_s"], np.float32))
    ds1 = bias_cols(np.asarray(inp["L1_ffn_down_s"], np.float32))

    x0 = np.asarray(inp["L0_attn_x"], np.float32)
    x_fm = [np.ascontiguousarray(x0[b].T).reshape(DC, P, T) for b in range(NCORES)]

    core_ids = list(range(NCORES))

    def run(nc, per_core, shared):
        in_maps = [dict(shared, **pc) for pc in per_core]
        return run_bass_kernel_spmd(nc, in_maps, core_ids=core_ids).results

    cc = {k: ch[k] for k in ("c_onesb", "c_onesrow", "c_cos", "c_sin")}
    cl = {k: ch[k] for k in ("c_onesb", "c_onesrow")}

    r = run(_cget("qkv0", lambda: build_qkv(0)), [{"x": x} for x in x_fm],
            dict(cc, wq=wq0, wk=wk0, wv=wv0))
    qkv = [(r[b]["q"], r[b]["k"], r[b]["v"]) for b in range(NCORES)]

    r = run(_cget("attn0", lambda: build_attncore(0)),
            [{"q": q, "k": k, "v": v} for (q, k, v) in qkv],
            {"iden": ch["iden"], "tri": ch["tri"]})
    aos = [r[b]["ao"] for b in range(NCORES)]

    r = run(_cget("wo0", lambda: build_woresid(0)),
            [{"ao": aos[b], "x": x_fm[b]} for b in range(NCORES)], {"wo": wo0})
    x1 = [r[b]["xo"] for b in range(NCORES)]

    r = run(_cget("ffn0", lambda: build_ffn(0)), [{"x": x1[b]} for b in range(NCORES)],
            dict(cl, wg=wg0, wu=wu0, wd=wd0))
    x2 = [r[b]["xo"] for b in range(NCORES)]

    r = run(_cget("qkv1", lambda: build_qkv(1)), [{"x": x2[b]} for b in range(NCORES)],
            dict(cc, wq=wq1, wk=wk1, wv=wv1, bq=bq1, bk=bk1, bv=bv1))
    qkv = [(r[b]["q"], r[b]["k"], r[b]["v"]) for b in range(NCORES)]

    r = run(_cget("attn1", lambda: build_attncore(1)),
            [{"q": q, "k": k, "v": v} for (q, k, v) in qkv],
            {"iden": ch["iden"], "tri": ch["tri"]})
    aos = [r[b]["ao"] for b in range(NCORES)]

    r = run(_cget("wo1", lambda: build_woresid(1)),
            [{"ao": aos[b], "x": x2[b]} for b in range(NCORES)],
            {"wo": wo1, "bo": bo1})
    x3 = [r[b]["xo"] for b in range(NCORES)]

    r = run(_cget("ffn1", lambda: build_ffn(1)), [{"x": x3[b]} for b in range(NCORES)],
            dict(cl, wg=wg1, wu=wu1, wd=wd1, gs=gs1, us=us1, ds=ds1))
    x4 = [r[b]["xo"] for b in range(NCORES)]

    out = np.empty((NCORES, T, D), dtype=np.float32)
    for b in range(NCORES):
        out[b] = x4[b].reshape(D, T).T
    return out


def _prep_all(inp):
    ch = _consts_host()
    rp = _rope_perm()
    p24 = _perm24()
    w = {}
    w["a0_wq"] = prep_w(col_perm_heads(np.asarray(inp["L0_attn_Wq"], np.float32), rp), TCW)
    w["a0_wk"] = prep_w(col_perm_heads(np.asarray(inp["L0_attn_Wk"], np.float32), rp), TCW)
    w["a0_wv"] = prep_w(np.asarray(inp["L0_attn_Wv"], np.float32), TCW)[0]
    w["a0_wo"] = prep_w(np.asarray(inp["L0_attn_Wo"], np.float32), TCW)
    w["f0_wg"] = prep_w(np.asarray(inp["L0_ffn_w_gate"], np.float32)[:, p24], TCW)
    w["f0_wu"] = prep_w(np.asarray(inp["L0_ffn_w_up"], np.float32)[:, p24], TCW)
    w["f0_wd"] = prep_w(np.asarray(inp["L0_ffn_w_down"], np.float32)[p24, :], P)
    w["a1_wq"] = prep_w(col_perm_heads(np.asarray(inp["L1_attn_Wq"], np.float32).T, rp), TCW)
    w["a1_wk"] = prep_w(col_perm_heads(np.asarray(inp["L1_attn_Wk"], np.float32).T, rp), TCW)
    w["a1_wv"] = prep_w(np.ascontiguousarray(np.asarray(inp["L1_attn_Wv"], np.float32).T), TCW)
    w["a1_wo"] = prep_w(np.ascontiguousarray(np.asarray(inp["L1_attn_Wo"], np.float32).T), TCW)
    w["a1_bq"] = bias_cols(col_perm_heads(np.asarray(inp["L1_attn_bq"], np.float32)[None, :], rp)[0])
    w["a1_bk"] = bias_cols(col_perm_heads(np.asarray(inp["L1_attn_bk"], np.float32)[None, :], rp)[0])
    w["a1_bv"] = np.ascontiguousarray(np.asarray(inp["L1_attn_bv"], np.float32)[None, :])
    w["a1_bo"] = bias_cols(np.asarray(inp["L1_attn_bo"], np.float32))
    w["f1_wg"] = prep_w(np.ascontiguousarray(np.asarray(inp["L1_ffn_gate_w"], np.float32).T), TCW)
    w["f1_wu"] = prep_w(np.ascontiguousarray(np.asarray(inp["L1_ffn_up_w"], np.float32).T), TCW)
    w["f1_wd"] = prep_w(np.ascontiguousarray(np.asarray(inp["L1_ffn_down_w"], np.float32).T), P)
    w["f1_gs"] = bias_cols(np.asarray(inp["L1_ffn_gate_s"], np.float32))
    w["f1_us"] = bias_cols(np.asarray(inp["L1_ffn_up_s"], np.float32))
    w["f1_ds"] = bias_cols(np.asarray(inp["L1_ffn_down_s"], np.float32))
    for k in ("c_onesb", "c_onesrow", "c_cos", "c_sin", "iden", "tri"):
        w[k] = ch[k]
    return w


def kernel(**inp):
    shared = _prep_all(inp)
    x0 = np.asarray(inp["L0_attn_x"], np.float32)
    x_fm = [np.ascontiguousarray(x0[b].T).reshape(DC, P, T) for b in range(NCORES)]
    nc = _cget("fused", build_fused)
    in_maps = [dict(shared, x=x_fm[b]) for b in range(NCORES)]
    res = run_bass_kernel_spmd(nc, in_maps, core_ids=list(range(NCORES)))
    out = np.empty((NCORES, T, D), dtype=np.float32)
    for b in range(NCORES):
        out[b] = res.results[b]["xo"].reshape(D, T).T
    return out


# version 16
# speedup vs baseline: 1.0479x; 1.0479x over previous
import math
import numpy as np
import ml_dtypes
from contextlib import ExitStack

import concourse.bass as bass
import concourse.mybir as mybir
from concourse.bass_utils import run_bass_kernel_spmd

F32 = mybir.dt.float32
BF16 = mybir.dt.bfloat16
AF = mybir.ActivationFunctionType
ALU = mybir.AluOpType
AX = mybir.AxisListType

P = 128
D = 2048
T = 2048
DC = D // P
TCW = 512
NTC = T // TCW
NTT = T // P
EPS = 1e-5
ISQ = 1.0 / math.sqrt(128.0)
NEG = -1.0e30

H0, KV0, DH = 16, 4, 128
META, WIN = 128, 1024
TK0 = META + WIN
I0 = 8192
IC0 = I0 // P
I1 = 5632
IC1 = I1 // P

BF = ml_dtypes.bfloat16
NCORES = 8



def _rope_perm():
    return np.concatenate([np.arange(0, DH, 2), np.arange(1, DH, 2)])


def _perm24():
    c = np.arange(I0 // P)[:, None]
    p = np.arange(P)[None, :]
    return (4 * ((c // 4) * P + p) + (c % 4)).reshape(-1)


def prep_w(Wf, S):
    Din, Dout = Wf.shape
    cin, ns = Din // P, Dout // S
    return np.ascontiguousarray(
        Wf.reshape(cin, P, ns, S).transpose(2, 1, 0, 3).reshape(ns, P, cin * S)
    ).astype(np.float32)


def rope_tables():
    inv = 1.0 / (10000.0 ** (np.arange(0, DH, 2, dtype=np.float64) / DH))
    ang = np.arange(T, dtype=np.float64)[None, :] * inv[:, None]
    return (np.cos(ang).astype(np.float32), np.sin(ang).astype(np.float32))


def col_perm_heads(Wf, perm, dh=DH):
    Dout = Wf.shape[1]
    nh = Dout // dh
    return np.ascontiguousarray(
        Wf.reshape(Wf.shape[0], nh, dh)[:, :, perm].reshape(Wf.shape[0], Dout))


def bias_cols(b):
    n = b.shape[0] // P
    return np.ascontiguousarray(b.reshape(n, P).T).astype(np.float32)



class PB:
    SEMS = ("gq", "sq", "pe", "dve", "act", "sf")

    def __init__(self, nc, es, npb=4, nwb=0):
        self.nc = nc
        self.es = es
        self.blk = es.enter_context(nc.Block())
        self.sem = {}
        self.cnt = {}
        for s in self.SEMS:
            self.sem[s] = es.enter_context(nc.semaphore("sem_" + s))
            self.cnt[s] = 0
        self.scopes = [es]
        self.prefix = ""
        self.pbank = []
        self.wbuf = []
        if npb or nwb:
            self.alloc_pool(npb, nwb)

    def sem_get(self, name):
        if name not in self.sem:
            self.sem[name] = self.es.enter_context(self.nc.semaphore("sem_" + name))
            self.cnt[name] = 0
        return self.sem[name]

    def sb(self, name, shape, dt):
        return self.scopes[-1].enter_context(
            self.nc.sbuf_tensor(self.prefix + name, shape, dt))

    def ps(self, name, shape, dt=F32):
        return self.scopes[-1].enter_context(
            self.nc.psum_tensor(self.prefix + name, shape, dt))

    def alloc_pool(self, npb, nwb):
        self.pbank = [self.ps(f"pbank{i}", [P, TCW], F32) for i in range(npb)]
        self.wbuf = [self.sb(f"wbuf{i}", [P, 8192], BF16) for i in range(nwb)]

    def push(self, prefix):
        self.scopes.append(ExitStack())
        self.prefix = prefix
        return self.scopes[-1]

    def pop(self):
        self.scopes.pop().close()
        self.pbank = []
        self.wbuf = []

    def inc(self, ins, s, n=1):
        ins.then_inc(self.sem[s], n)
        self.cnt[s] += n
        return self.cnt[s]

    def dma(self, eng, dst, src, sem=None):
        if sem is None:
            sem = "gq" if eng is self.nc.gpsimd else "sq"
        else:
            self.sem_get(sem)
        ins = eng.dma_start(out=dst, in_=src)
        return self.inc(ins, sem, 16)

    def wait(self, eng, s, tok):
        if tok > 0:
            eng.wait_ge(self.sem[s], tok)

    def self_edge(self, eng, ins):
        tok = self.inc(ins, "sf")
        eng.wait_ge(self.sem["sf"], tok)

    def barrier(self):
        snap = dict(self.cnt)
        for bname, own in (("gpsimd", "gq"), ("sync", "sq"), ("tensor", "pe"),
                           ("vector", "dve"), ("scalar", "act")):
            def mk(own_s):
                def f(eng):
                    for s, v in snap.items():
                        if s != own_s and s != "sf" and v > 0:
                            eng.wait_ge(self.sem[s], v)
                return f
            getattr(self.blk, bname)(mk(own))

    def wait_all_dma(self, eng):
        for s, v in self.cnt.items():
            if s not in ("pe", "dve", "act", "sf") and v > 0:
                eng.wait_ge(self.sem[s], v)



def mk_ln_bufs(pb, ping=True):
    lb = {}
    lb["ping"] = ping
    lb["xb"] = pb.sb("ln_xb", [P, (2 if ping else 1) * DC * TCW], BF16)
    lb["sq"] = pb.sb("ln_sq", [P, DC * TCW], BF16)
    lb["onesb"] = pb.sb("ln_ones", [P, 1], BF16)
    lb["ones_row"] = pb.sb("ln_onesrow", [1, P], F32)
    for k, shape, dt in (("m", [1, TCW], F32), ("v", [1, TCW], F32),
                         ("r", [1, TCW], F32), ("a", [1, TCW], F32),
                         ("t", [1, TCW], F32), ("rb", [P, TCW], BF16),
                         ("ab", [P, TCW], BF16), ("t2", [P, TCW], BF16)):
        lb[k] = pb.sb("ln_" + k, shape, dt)
    return lb


def emit_ln(pb, x_dram, lnb, tcs, xn_ap):
    xb, sq = lnb["xb"], lnb["sq"]
    png = 1 if lnb.get("ping", True) else 0
    m_sb, v_sb, r_sb, a_sb, t_sb = (lnb[k] for k in ("m", "v", "r", "a", "t"))
    rb, ab, t2 = lnb["rb"], lnb["ab"], lnb["t2"]

    pb.sem_get("lnx0")
    pb.sem_get("lnx1")
    gq0p = [pb.cnt["lnx0"], pb.cnt["lnx1"]]
    pe0 = pb.cnt["pe"]
    act0 = pb.cnt["act"]
    dve0 = pb.cnt["dve"]
    n = len(tcs)
    gq_tok = [gq0p[k % 2] + 256 * (k // 2 + 1) for k in range(n)]
    act_sq = [act0 + 2 * k + 1 for k in range(n)]
    act_std = [act0 + 2 * k + 2 for k in range(n)]
    pe_sums = [pe0 + 2 * k + 1 for k in range(n)]
    pe_bc = [pe0 + 2 * k + 2 for k in range(n)]
    dve_var = [dve0 + 6 * k + 4 for k in range(n)]
    dve_r = [dve0 + 6 * k + 5 for k in range(n)]
    dve_xn = [dve0 + 6 * k + 6 for k in range(n)]

    @pb.blk.gpsimd
    def _(g):
        for k, tc in enumerate(tcs):
            if k >= 2 - (1 - png):
                g.wait_ge(pb.sem["dve"], dve_xn[k - 2 + (1 - png)])
            half = (k % 2) * DC * TCW * png
            for c in range(DC):
                tok = pb.dma(g, xb[:, half + c * TCW: half + (c + 1) * TCW],
                             x_dram[c][:, tc * TCW:(tc + 1) * TCW],
                             sem=f"lnx{k % 2}")
            assert tok == gq_tok[k]

    @pb.blk.scalar
    def _(sc):
        for k, tc in enumerate(tcs):
            sc.wait_ge(pb.sem[f"lnx{k % 2}"], gq_tok[k])
            if k >= 1:
                sc.wait_ge(pb.sem["pe"], pe_sums[k - 1])
            half = (k % 2) * DC * TCW * png
            ins = sc.activation(sq[:, :], xb[:, half: half + DC * TCW], AF.Square)
            tok = pb.inc(ins, "act")
            assert tok == act_sq[k]
            sc.wait_ge(pb.sem["dve"], dve_var[k])
            ins = sc.activation(t_sb[0:1, :], v_sb[0:1, :], AF.Sqrt)
            tok = pb.inc(ins, "act")
            assert tok == act_std[k]

    @pb.blk.tensor
    def _(te):
        for k, tc in enumerate(tcs):
            te.wait_ge(pb.sem["act"], act_sq[k])
            if k >= 1:
                te.wait_ge(pb.sem["dve"], dve_var[k - 1])
            half = (k % 2) * DC * TCW * png
            for c in range(DC):
                te.matmul(pb.pbank[0][0:1, :], lnb["onesb"][:, 0:1],
                          xb[:, half + c * TCW: half + (c + 1) * TCW],
                          start=(c == 0), stop=(c == DC - 1))
            for c in range(DC):
                ins = te.matmul(pb.pbank[1][0:1, :], lnb["onesb"][:, 0:1],
                                sq[:, c * TCW:(c + 1) * TCW],
                                start=(c == 0), stop=(c == DC - 1))
            tok = pb.inc(ins, "pe")
            assert tok == pe_sums[k]
            te.wait_ge(pb.sem["dve"], dve_r[k])
            if k >= 1:
                te.wait_ge(pb.sem["dve"], dve_xn[k - 1])
            te.matmul(pb.pbank[2][:, :], lnb["ones_row"][0:1, :], r_sb[0:1, :],
                      start=True, stop=True)
            ins = te.matmul(pb.pbank[3][:, :], lnb["ones_row"][0:1, :], a_sb[0:1, :],
                            start=True, stop=True)
            tok = pb.inc(ins, "pe")
            assert tok == pe_bc[k]

    @pb.blk.vector
    def _(v):
        for k, tc in enumerate(tcs):
            v.wait_ge(pb.sem["pe"], pe_sums[k])
            ins = v.tensor_scalar_mul(m_sb[0:1, :], pb.pbank[0][0:1, :], 1.0 / D)
            v.wait_ge(pb.sem["dve"], pb.inc(ins, "dve"))
            ins = v.tensor_tensor(t_sb[0:1, :], m_sb[0:1, :], m_sb[0:1, :], ALU.mult)
            v.wait_ge(pb.sem["dve"], pb.inc(ins, "dve"))
            ins = v.scalar_tensor_tensor(v_sb[0:1, :], pb.pbank[1][0:1, :], 1.0 / D,
                                          t_sb[0:1, :], ALU.mult, ALU.subtract)
            v.wait_ge(pb.sem["dve"], pb.inc(ins, "dve"))
            ins = v.tensor_scalar_add(v_sb[0:1, :], v_sb[0:1, :], EPS)
            tok = pb.inc(ins, "dve")
            assert tok == dve_var[k]
            v.wait_ge(pb.sem["act"], act_std[k])
            pb.self_edge(v, v.reciprocal(r_sb[0:1, :], t_sb[0:1, :]))
            ins = v.tensor_tensor(a_sb[0:1, :], m_sb[0:1, :], r_sb[0:1, :], ALU.mult)
            tok = pb.inc(ins, "dve")
            assert tok == dve_r[k]
            v.wait_ge(pb.sem["pe"], pe_bc[k])
            v.tensor_copy(rb[:, :], pb.pbank[2][:, :])
            v.tensor_copy(ab[:, :], pb.pbank[3][:, :])
            half = (k % 2) * DC * TCW * png
            for c in range(DC):
                v.tensor_tensor(t2[:, :], xb[:, half + c * TCW: half + (c + 1) * TCW],
                                rb[:, :], ALU.mult)
                ins = v.tensor_tensor(xn_ap(c, tc), t2[:, :], ab[:, :], ALU.subtract)
            tok = pb.inc(ins, "dve")
            assert tok == dve_xn[k]



def emit_proj_fm(pb, wslices, n_slices, cin, S, rhs_ap, evac_emit, ev_guard=None,
                 nwb=4):
    osubs = S // P
    gps = osubs * NTC
    pe0 = pb.cnt["pe"]
    ev0 = pb.cnt["act"]
    NB = 4
    gq_toks = []

    @pb.blk.gpsimd
    def _(g):
        for s in range(n_slices):
            if s >= nwb:
                g.wait_ge(pb.sem["pe"], pe0 + (s - nwb + 1) * gps)
            tok = pb.dma(g, pb.wbuf[s % nwb][:, 0:cin * S], wslices[s],
                         sem=f"w{s % nwb}")
            gq_toks.append(tok)

    groups = [(s, osub, tc) for s in range(n_slices) for osub in range(osubs)
              for tc in range(NTC)]

    @pb.blk.tensor
    def _(te):
        for gi, (s, osub, tc) in enumerate(groups):
            if osub == 0 and tc == 0:
                te.wait_ge(pb.sem_get(f"w{s % nwb}"), gq_toks[s])
            bank = pb.pbank[gi % NB]
            if gi >= NB:
                te.wait_ge(pb.sem["act"], ev0 + (gi - NB + 1))
            for c in range(cin):
                ins = te.matmul(bank[:, :],
                                pb.wbuf[s % nwb][:, c * S + osub * P: c * S + osub * P + P],
                                rhs_ap(c, tc), start=(c == 0), stop=(c == cin - 1))
            pb.inc(ins, "pe")

    @pb.blk.scalar
    def _(sc):
        for gi, (s, osub, tc) in enumerate(groups):
            sc.wait_ge(pb.sem["pe"], pe0 + gi + 1)
            if ev_guard is not None:
                ev_guard(sc, gi)
            ins = evac_emit(sc, gi, s, osub, tc, pb.pbank[gi % NB])
            pb.inc(ins, "act")

    return groups, ev0


def emit_rope_store(pb, groups, ev_tok_fn, cosT, sinT, ev, rA, rB, st, store_dst_fn):
    dve0 = pb.cnt["dve"]
    pb.sem_get("y0")
    pb.sem_get("y1")
    y0 = [pb.cnt["y0"], pb.cnt["y1"]]

    @pb.blk.vector
    def _(v):
        for gi, g in enumerate(groups):
            tc = g[2]
            e = ev[gi % 2]
            d = st[gi % 2]
            v.wait_ge(pb.sem["act"], ev_tok_fn(gi))
            if gi >= 2:
                pb.wait(v, f"y{gi % 2}", y0[gi % 2] + 16 * (gi // 2))
            csl = cosT[0:64, tc * TCW:(tc + 1) * TCW]
            csh = cosT[64:128, tc * TCW:(tc + 1) * TCW]
            snl = sinT[0:64, tc * TCW:(tc + 1) * TCW]
            snh = sinT[64:128, tc * TCW:(tc + 1) * TCW]
            v.tensor_tensor(rA[:, :], e[0:64, :], csl, ALU.mult)
            v.tensor_tensor(rB[:, :], e[64:128, :], snh, ALU.mult)
            v.tensor_tensor(d[0:64, :], rA[:, :], rB[:, :], ALU.subtract)
            v.tensor_tensor(rA[:, :], e[0:64, :], snl, ALU.mult)
            v.tensor_tensor(rB[:, :], e[64:128, :], csh, ALU.mult)
            ins = v.tensor_tensor(d[64:128, :], rA[:, :], rB[:, :], ALU.add)
            pb.inc(ins, "dve")

    @pb.blk.sync
    def _(sy):
        for gi, g in enumerate(groups):
            sy.wait_ge(pb.sem["dve"], dve0 + gi + 1)
            pb.dma(sy, store_dst_fn(g), st[gi % 2][:, :], sem=f"y{gi % 2}")
    return dve0



def emit_resid_stream(pb, ngroups, pe0, bank_fn, x_src_fn, xo_dst_fn, dve_op_fn,
                      xt, xo):
    dve0 = pb.cnt["dve"]
    nxt = len(xt)
    for i in range(nxt):
        pb.sem_get(f"x{i}")
    for i in range(2):
        pb.sem_get(f"y{i}")
    ld_toks = [None] * ngroups
    st_toks = [None] * ngroups

    @pb.blk.sync
    def _(sy):
        for g in range(ngroups):
            if g >= nxt:
                sy.wait_ge(pb.sem["dve"], dve0 + g - nxt + 1)
            ld_toks[g] = pb.dma(sy, xt[g % nxt][:, :], x_src_fn(g), sem=f"x{g % nxt}")
            if g >= 2:
                sy.wait_ge(pb.sem["dve"], dve0 + g - 1)
                st_toks[g - 2] = pb.dma(sy, xo_dst_fn(g - 2), xo[(g - 2) % 2][:, :],
                                        sem=f"y{(g - 2) % 2}")
        for j in (ngroups - 2, ngroups - 1):
            sy.wait_ge(pb.sem["dve"], dve0 + j + 1)
            st_toks[j] = pb.dma(sy, xo_dst_fn(j), xo[j % 2][:, :], sem=f"y{j % 2}")

    @pb.blk.vector
    def _(v):
        for g in range(ngroups):
            v.wait_ge(pb.sem["pe"], pe0 + g + 1)
            v.wait_ge(pb.sem[f"x{g % nxt}"], ld_toks[g])
            if g >= 2:
                v.wait_ge(pb.sem[f"y{g % 2}"], st_toks[g - 2])
            ins = dve_op_fn(v, g, xo[g % 2], bank_fn(g), xt[g % nxt])
            pb.inc(ins, "dve")
    return dve0



def emit_qkv(pb, layer, io):
    x_d = io["x"]
    wq_d, wk_d, wv_d = io["wq"], io["wk"], io["wv"]
    onesb_d, onesrow_d = io["c_onesb"], io["c_onesrow"]
    cos_d, sin_d = io["c_cos"], io["c_sin"]
    q_d, k_d, v_d = io["q"], io["k"], io["v"]
    if layer == 1:
        bq_d, bk_d, bv_d = io["bq"], io["bk"], io["bv"]
    k_slices = 1 if layer == 0 else 4
    if True:
        pb.alloc_pool(4, 3)
        lnb = mk_ln_bufs(pb)
        xn = pb.sb("xn", [P, DC * T], BF16)
        cosT = pb.sb("cosT", [P, T], BF16)
        sinT = pb.sb("sinT", [P, T], BF16)
        ev = [pb.sb(f"ev{i}", [P, TCW], BF16) for i in range(2)]
        rA = pb.sb("rA", [64, TCW], BF16)
        rB = pb.sb("rB", [64, TCW], BF16)
        st = [pb.sb(f"st{i}", [P, TCW], BF16) for i in range(2)]
        if layer == 1:
            bq_sb = pb.sb("bq_s", [P, 16], F32)
            bk_sb = pb.sb("bk_s", [P, 16], F32)
            bv_sb = pb.sb("bv_s", [1, D], F32)
            bvb = pb.sb("bvb", [P, D], F32)

        @pb.blk.sync
        def _(sy):
            pb.dma(sy, lnb["onesb"][:, 0:1], onesb_d)
            pb.dma(sy, lnb["ones_row"][0:1, :], onesrow_d)
            pb.dma(sy, cosT[:, :], cos_d)
            pb.dma(sy, sinT[:, :], sin_d)
            if layer == 1:
                pb.dma(sy, bq_sb[:, :], bq_d)
                pb.dma(sy, bk_sb[:, :], bk_d)
                pb.dma(sy, bv_sb[0:1, :], bv_d)
        pb.barrier()

        emit_ln(pb, x_d, lnb, list(range(NTC)),
                lambda c, tc: xn[:, c * T + tc * TCW: c * T + (tc + 1) * TCW])
        pb.barrier()

        def rhs_ap(c, tc):
            return xn[:, c * T + tc * TCW: c * T + (tc + 1) * TCW]

        def mk_qk_evac(bias_sb):
            def _evac(eng, gi, s, osub, tc, ps):
                head = s * 4 + osub
                if bias_sb is not None:
                    return eng.activation(ev[gi % 2][:, :], ps[:, :], AF.Identity,
                                          bias=bias_sb[:, head:head + 1])
                return eng.copy(ev[gi % 2][:, :], ps[:, :])
            return _evac

        rope_base = pb.cnt["dve"]

        def ev_guard(eng, gi):
            pb.wait(eng, "dve", rope_base + gi - 1)

        groups, ev0 = emit_proj_fm(pb, wq_d, 4, DC, TCW, rhs_ap,
                                   mk_qk_evac(bq_sb if layer == 1 else None),
                                   ev_guard=ev_guard, nwb=3)
        emit_rope_store(pb, groups, lambda gi: ev0 + gi + 1, cosT, sinT, ev, rA, rB,
                        st, lambda g: q_d[g[2], g[0] * 4 + g[1]])
        pb.barrier()

        rope_base = pb.cnt["dve"]
        groups, ev0 = emit_proj_fm(pb, wk_d, k_slices, DC, TCW, rhs_ap,
                                   mk_qk_evac(bk_sb if layer == 1 else None),
                                   ev_guard=ev_guard, nwb=3)
        emit_rope_store(pb, groups, lambda gi: ev0 + gi + 1, cosT, sinT, ev, rA, rB,
                        st, lambda g: k_d[g[0] * 4 + g[1]][:, g[2] * TCW:(g[2] + 1) * TCW])
        pb.barrier()

        if layer == 1:
            pe0b = pb.cnt["pe"]
            dve0b = pb.cnt["dve"]

            @pb.blk.tensor
            def _(te):
                for sdv in range(4):
                    if sdv >= 1:
                        te.wait_ge(pb.sem["dve"], dve0b + sdv)
                    ins = te.matmul(pb.pbank[0][:, :], lnb["ones_row"][0:1, :],
                                    bv_sb[0:1, sdv * TCW:(sdv + 1) * TCW],
                                    start=True, stop=True)
                    pb.inc(ins, "pe")

            @pb.blk.vector
            def _(v):
                for sdv in range(4):
                    v.wait_ge(pb.sem["pe"], pe0b + sdv + 1)
                    ins = v.tensor_copy(bvb[:, sdv * TCW:(sdv + 1) * TCW],
                                        pb.pbank[0][:, :])
                    pb.inc(ins, "dve")
            pb.barrier()

        n_dv = 1 if layer == 0 else 4
        dve_v0 = pb.cnt["dve"]
        pe_v0 = pb.cnt["pe"]
        for sdv in range(n_dv):
            pe0 = pb.cnt["pe"]
            dve0v = pb.cnt["dve"]
            tok_wv = [None]

            @pb.blk.gpsimd
            def _(g):
                pb.wait(g, "pe", pe_v0 + NTT * (sdv - 1))
                tok_wv[0] = pb.dma(g, pb.wbuf[sdv % 2][:, 0:DC * TCW],
                                   wv_d if layer == 0 else wv_d[sdv],
                                   sem=f"w{sdv % 2}")

            @pb.blk.tensor
            def _(te):
                te.wait_ge(pb.sem[f"w{sdv % 2}"], tok_wv[0])
                for tt in range(NTT):
                    bank = pb.pbank[tt % 4]
                    Gg = sdv * NTT + tt
                    if Gg >= 4:
                        te.wait_ge(pb.sem["dve"], dve_v0 + Gg - 3)
                    for c in range(DC):
                        ins = te.matmul(bank[:, :],
                                        xn[:, c * T + tt * P: c * T + (tt + 1) * P],
                                        pb.wbuf[sdv % 2][:, c * TCW:(c + 1) * TCW],
                                        start=(c == 0), stop=(c == DC - 1))
                    pb.inc(ins, "pe")

            pb.sem_get("y0")
            pb.sem_get("y1")
            y0v = [pb.cnt["y0"], pb.cnt["y1"]]

            @pb.blk.vector
            def _(v):
                for tt in range(NTT):
                    v.wait_ge(pb.sem["pe"], pe0 + tt + 1)
                    ytok = y0v[tt % 2] + 16 * (tt // 2) if tt >= 2 else y0v[tt % 2]
                    pb.wait(v, f"y{tt % 2}", ytok)
                    if layer == 1:
                        ins = v.tensor_tensor(st[tt % 2][:, :], pb.pbank[tt % 4][:, :],
                                              bvb[:, sdv * TCW:(sdv + 1) * TCW], ALU.add)
                    else:
                        ins = v.tensor_copy(st[tt % 2][:, :], pb.pbank[tt % 4][:, :])
                    pb.inc(ins, "dve")

            @pb.blk.sync
            def _(sy):
                for tt in range(NTT):
                    sy.wait_ge(pb.sem["dve"], dve0v + tt + 1)
                    if layer == 0:
                        pb.dma(sy, v_d[tt], st[tt % 2][:, :], sem=f"y{tt % 2}")
                    else:
                        pb.dma(sy, v_d[tt][:, sdv * TCW:(sdv + 1) * TCW],
                               st[tt % 2][:, :], sem=f"y{tt % 2}")


def io_qkv(nc, layer, pfx="", scratch=()):
    nh_k = KV0 if layer == 0 else 16
    k_slices = 1 if layer == 0 else 4
    vdw = TCW if layer == 0 else D
    kind = lambda n: "Internal" if n in scratch else (
        "ExternalOutput" if n in ("q", "k", "v") else "ExternalInput")
    io = {}
    io["x"] = nc.dram_tensor(pfx + "x", [DC, P, T], F32, kind=kind("x")).ap()
    io["wq"] = nc.dram_tensor(pfx + "wq", [4, P, DC * TCW], F32, kind="ExternalInput").ap()
    io["wk"] = nc.dram_tensor(pfx + "wk", [k_slices, P, DC * TCW], F32, kind="ExternalInput").ap()
    io["wv"] = nc.dram_tensor(pfx + "wv", [P, DC * TCW] if layer == 0 else [4, P, DC * TCW],
                              F32, kind="ExternalInput").ap()
    io["q"] = nc.dram_tensor(pfx + "q", [NTC, 16, P, TCW], BF16, kind=kind("q")).ap()
    io["k"] = nc.dram_tensor(pfx + "k", [nh_k, P, T], BF16, kind=kind("k")).ap()
    io["v"] = nc.dram_tensor(pfx + "v", [NTT, P, vdw], BF16, kind=kind("v")).ap()
    if layer == 1:
        io["bq"] = nc.dram_tensor(pfx + "bq", [P, 16], F32, kind="ExternalInput").ap()
        io["bk"] = nc.dram_tensor(pfx + "bk", [P, 16], F32, kind="ExternalInput").ap()
        io["bv"] = nc.dram_tensor(pfx + "bv", [1, D], F32, kind="ExternalInput").ap()
    return io


def _io_consts(nc, rope=True):
    io = {}
    io["c_onesb"] = nc.dram_tensor("c_onesb", [P, 1], BF16, kind="ExternalInput").ap()
    io["c_onesrow"] = nc.dram_tensor("c_onesrow", [1, P], F32, kind="ExternalInput").ap()
    if rope:
        io["c_cos"] = nc.dram_tensor("c_cos", [P, T], BF16, kind="ExternalInput").ap()
        io["c_sin"] = nc.dram_tensor("c_sin", [P, T], BF16, kind="ExternalInput").ap()
    return io


def build_qkv(layer):
    nc = bass.Bass("TRN2", target_bir_lowering=False)
    es = ExitStack()
    io = io_qkv(nc, layer)
    io.update(_io_consts(nc))
    with es:
        pb = PB(nc, es, npb=0, nwb=0)
        emit_qkv(pb, layer, io)

        @pb.blk.sync
        def _(sy):
            pb.wait_all_dma(sy)
    return nc



def _score_ranges_l0(qt):
    jmax = min(TK0, qt * P + P)
    pieces = []
    for bank_lo in (0, 512, 1024):
        lo = bank_lo
        hi = min(bank_lo + 512, jmax)
        while lo < hi:
            if lo < META:
                seg_hi = min(hi, META)
                pieces.append((lo, seg_hi - lo, lo))
                lo = seg_hi
            else:
                pieces.append((lo, hi - lo, lo + WIN - META))
                lo = hi
    return pieces, jmax


def _score_ranges_l1(qt):
    jmax = qt * P + P
    pieces = []
    lo = 0
    while lo < jmax:
        hi = min(lo + 512, jmax)
        pieces.append((lo, hi - lo, lo))
        lo = hi
    return pieces, jmax


def _prev_key(tc, h, qtl):
    if qtl > 0:
        return (tc, h, qtl - 1)
    if h > 0:
        return (tc, h - 1, 3)
    if tc > 0:
        return (tc - 1, 15, 3)
    return None


def emit_attn(pb, layer, io):
    nh_k = KV0 if layer == 0 else 16
    vdw = TCW if layer == 0 else D
    q_d, k_d, v_d = io["q"], io["k"], io["v"]
    iden_d, tri_d, ao_d = io["iden"], io["tri"], io["ao"]

    TKmax = TK0 if layer == 0 else T
    nbank_s = (TKmax * 4 + 2047) // 2048
    rng = _score_ranges_l0 if layer == 0 else _score_ranges_l1

    if True:
        k_sb = pb.sb("k_sb", [P, nh_k * T], BF16)
        v_sb = pb.sb("v_sb", [P, NTT * vdw], BF16)
        q_sb = pb.sb("q_sb", [P, 16 * TCW], BF16)
        iden = pb.sb("iden_s", [P, P], BF16)
        tri = pb.sb("tri_s", [P, P], F32)
        Pm = pb.sb("Pm", [P, TKmax], BF16)
        P2 = pb.sb("P2", [P, TKmax], BF16)
        PT = pb.sb("PT", [P, TKmax], BF16)
        ao = pb.sb("ao_s", [P, 16 * TCW], BF16)
        rmax = pb.sb("rmax", [P, 1], F32)
        bias = pb.sb("bias", [P, 1], F32)
        rs = pb.sb("rs", [P, 1], F32)
        rdiv = pb.sb("rdiv", [P, 1], F32)

        S_ps = pb.ps("S_ps", [P, nbank_s * TCW], F32)
        PT_ps = pb.ps("PT_ps", [P, 2048], BF16)
        av_ps = pb.ps("av_ps", [P, TCW], F32)

        pe_count = pb.cnt["pe"]
        act_count = pb.cnt["act"]
        dve_count = pb.cnt["dve"]
        pe_sc_tok, pe_tr_tok, pe_av_tok = {}, {}, {}
        act_exp_tok, act_pt_tok, act_ao_tok = {}, {}, {}
        dve_mx_tok, dve_nm_tok = {}, {}
        pe_tc_done = {}
        for tc in range(NTC):
            for h in range(16):
                for qtl in range(4):
                    key = (tc, h, qtl)
                    pe_count += 1
                    pe_sc_tok[key] = pe_count
                    dve_count += 1
                    dve_mx_tok[key] = dve_count
                    act_count += 1
                    act_exp_tok[key] = act_count
                    dve_count += 1
                    dve_nm_tok[key] = dve_count
                    pe_count += 1
                    pe_tr_tok[key] = pe_count
                    act_count += 1
                    act_pt_tok[key] = act_count
                    pe_count += 1
                    pe_av_tok[key] = pe_count
                act_count += 1
                act_ao_tok[(tc, h)] = act_count
            pe_tc_done[tc] = pe_count

        @pb.blk.sync
        def _(sy):
            pb.dma(sy, iden[:, :], iden_d)
            pb.dma(sy, tri[:, :], tri_d)
            for g in range(nh_k):
                pb.dma(sy, k_sb[:, g * T:(g + 1) * T], k_d[g])
            for tt in range(NTT):
                pb.dma(sy, v_sb[:, tt * vdw:(tt + 1) * vdw], v_d[tt])
        base_loads = pb.cnt["sq"]

        q_toks = []

        @pb.blk.gpsimd
        def _(g):
            for tc in range(NTC):
                if tc >= 1:
                    g.wait_ge(pb.sem["pe"], pe_tc_done[tc - 1])
                for h in range(16):
                    tok = pb.dma(g, q_sb[:, h * TCW:(h + 1) * TCW], q_d[tc, h],
                                 sem="qld")
                q_toks.append(tok)

        @pb.blk.tensor
        def _(te):
            te.wait_ge(pb.sem["sq"], base_loads)
            for tc in range(NTC):
                te.wait_ge(pb.sem_get("qld"), q_toks[tc])
                for h in range(16):
                    g = h // 4 if layer == 0 else h
                    for qtl in range(4):
                        qt = tc * 4 + qtl
                        pieces, jmax = rng(qt)
                        nkc = (jmax + P - 1) // P
                        key = (tc, h, qtl)
                        prev = _prev_key(tc, h, qtl)
                        if prev is not None:
                            te.wait_ge(pb.sem["act"], act_exp_tok[prev])
                        lhsq = q_sb[:, h * TCW + qtl * P: h * TCW + (qtl + 1) * P]
                        for (slo, n, klo) in pieces:
                            ins = te.matmul(S_ps[:, slo:slo + n], lhsq,
                                            k_sb[:, g * T + klo: g * T + klo + n],
                                            start=True, stop=True)
                        assert pb.inc(ins, "pe") == pe_sc_tok[key]
                        te.wait_ge(pb.sem["dve"], dve_nm_tok[key])
                        if prev is not None:
                            te.wait_ge(pb.sem["act"], act_pt_tok[prev])
                        for kc in range(nkc):
                            ins = te.transpose(PT_ps[:, kc * P:(kc + 1) * P],
                                               P2[:, kc * P:(kc + 1) * P], iden[:, :])
                        assert pb.inc(ins, "pe") == pe_tr_tok[key]
                        te.wait_ge(pb.sem["act"], act_pt_tok[key])
                        if qtl == 0:
                            pk = (tc, h - 1) if h > 0 else ((tc - 1, 15) if tc > 0 else None)
                            if pk is not None:
                                te.wait_ge(pb.sem["act"], act_ao_tok[pk])
                        for kc in range(nkc):
                            vt = (0 if kc == 0 else kc + 7) if layer == 0 else kc
                            vcol = (g if layer == 0 else h) * P
                            ins = te.matmul(av_ps[:, qtl * P:(qtl + 1) * P],
                                            v_sb[:, vt * vdw + vcol: vt * vdw + vcol + P],
                                            PT[:, kc * P:(kc + 1) * P],
                                            start=(kc == 0), stop=(kc == nkc - 1))
                        assert pb.inc(ins, "pe") == pe_av_tok[key]

        @pb.blk.vector
        def _(v):
            for tc in range(NTC):
                for h in range(16):
                    for qtl in range(4):
                        qt = tc * 4 + qtl
                        pieces, jmax = rng(qt)
                        key = (tc, h, qtl)
                        prev = _prev_key(tc, h, qtl)
                        v.wait_ge(pb.sem["pe"], pe_sc_tok[key])
                        if jmax == qt * P + P:
                            v.tensor_tensor(S_ps[:, jmax - P: jmax],
                                            S_ps[:, jmax - P: jmax], tri[:, :], ALU.add)
                        pb.self_edge(v, v.reduce_max(rmax[:, 0:1],
                                                      S_ps[:, 0:jmax], AX.X))
                        ins = v.tensor_scalar_mul(bias[:, 0:1], rmax[:, 0:1], -ISQ)
                        assert pb.inc(ins, "dve") == dve_mx_tok[key]
                        v.wait_ge(pb.sem["act"], act_exp_tok[key])
                        if prev is not None:
                            v.wait_ge(pb.sem["pe"], pe_tr_tok[prev])
                        pb.self_edge(v, v.reciprocal(rdiv[:, 0:1], rs[:, 0:1]))
                        ins = v.tensor_scalar_mul(P2[:, 0:jmax], Pm[:, 0:jmax],
                                                  rdiv[:, 0:1])
                        assert pb.inc(ins, "dve") == dve_nm_tok[key]

        pb.sem_get("aos")
        aos0 = pb.cnt["aos"]

        @pb.blk.scalar
        def _(sc):
            for tc in range(NTC):
                for h in range(16):
                    for qtl in range(4):
                        qt = tc * 4 + qtl
                        pieces, jmax = rng(qt)
                        nkc = (jmax + P - 1) // P
                        key = (tc, h, qtl)
                        sc.wait_ge(pb.sem["dve"], dve_mx_tok[key])
                        ins = sc.activation(Pm[:, 0:jmax], S_ps[:, 0:jmax], AF.Exp,
                                            bias=bias[:, 0:1], scale=ISQ,
                                            accum_out=rs[:, 0:1])
                        assert pb.inc(ins, "act") == act_exp_tok[key]
                        sc.wait_ge(pb.sem["pe"], pe_tr_tok[key])
                        ins = sc.copy(PT[:, 0:nkc * P], PT_ps[:, 0:nkc * P])
                        assert pb.inc(ins, "act") == act_pt_tok[key]
                    if h == 0 and tc >= 1:
                        sc.wait_ge(pb.sem_get("aos"), aos0 + 16 * tc)
                    sc.wait_ge(pb.sem["pe"], pe_av_tok[(tc, h, 3)])
                    ins = sc.copy(ao[:, h * TCW:(h + 1) * TCW], av_ps[:, :])
                    assert pb.inc(ins, "act") == act_ao_tok[(tc, h)]

        @pb.blk.sync
        def _(sy):
            for tc in range(NTC):
                sy.wait_ge(pb.sem["act"], act_ao_tok[(tc, 15)])
                pb.dma(sy, ao_d[tc], ao[:, :], sem="aos")


def io_attn(nc, layer, pfx="", scratch=()):
    nh_k = KV0 if layer == 0 else 16
    vdw = TCW if layer == 0 else D
    kind = lambda n, out: "Internal" if n in scratch else (
        "ExternalOutput" if out else "ExternalInput")
    io = {}
    io["q"] = nc.dram_tensor(pfx + "q", [NTC, 16, P, TCW], BF16, kind=kind("q", False)).ap()
    io["k"] = nc.dram_tensor(pfx + "k", [nh_k, P, T], BF16, kind=kind("k", False)).ap()
    io["v"] = nc.dram_tensor(pfx + "v", [NTT, P, vdw], BF16, kind=kind("v", False)).ap()
    io["ao"] = nc.dram_tensor(pfx + "ao", [NTC, P, 16 * TCW], BF16, kind=kind("ao", True)).ap()
    return io


def _chunks_of(layer, qt):
    rngf = _score_ranges_l0 if layer == 0 else _score_ranges_l1
    pieces, jmax = rngf(qt)
    out = []
    lo = 0
    while lo < jmax:
        hi = min(lo + 512, jmax)
        rel = []
        for (slo, n, klo) in pieces:
            a = max(slo, lo)
            b = min(slo + n, hi)
            if a < b:
                rel.append((a - lo, b - a, klo + (a - slo)))
        out.append((lo, hi, rel, hi == jmax and jmax == qt * P + P))
        lo = hi
    return out, jmax


def emit_attn2(pb, layer, io):
    nh_k = KV0 if layer == 0 else 16
    vdw = TCW if layer == 0 else D
    q_d, k_d, v_d = io["q"], io["k"], io["v"]
    iden_d, tri_d, ao_d = io["iden"], io["tri"], io["ao"]
    TKmax = TK0 if layer == 0 else T
    DP = 2

    k_sb = pb.sb("k_sb", [P, nh_k * T], BF16)
    v_sb = pb.sb("v_sb", [P, NTT * vdw], BF16)
    q_sb = pb.sb("q_sb", [P, 16 * TCW], BF16)
    iden = pb.sb("iden_s", [P, P], BF16)
    tri = pb.sb("tri_s", [P, P], BF16)
    Pm = [pb.sb(f"Pm{i}", [P, TKmax], BF16) for i in range(2)]
    P2 = [pb.sb(f"P2{i}", [P, TKmax], BF16) for i in range(2)]
    PT = pb.sb("PT", [P, TKmax], BF16)
    ao = pb.sb("ao_s", [P, 16 * TCW], BF16)
    nm = [pb.sb(f"nm{i}", [P, 4], F32) for i in range(2)]
    sa = [pb.sb(f"sa{i}", [P, 4], F32) for i in range(2)]
    fa = [pb.sb(f"fa{i}", [P, 4], F32) for i in range(2)]
    sc_all = [pb.sb(f"sc{i}", [P, 4], F32) for i in range(2)]
    ngm = [pb.sb(f"ngm{i}", [P, 1], F32) for i in range(2)]
    stot = pb.sb("stot", [P, 1], F32)
    rdiv = pb.sb("rdiv", [P, 1], F32)
    ssc = pb.sb("ssc", [P, 4], F32)

    S_ps = [pb.ps(f"S_ps{i}", [P, TCW], F32) for i in range(4)]
    PT_ps = pb.ps("PT_ps", [P, 2048], BF16)
    av_ps = pb.ps("av_ps", [P, TCW], F32)

    blocks = [(tc, h, qtl) for tc in range(NTC) for h in range(16)
              for qtl in range(4)]
    nblk = len(blocks)
    binfo = []
    for (tc, h, qtl) in blocks:
        ch, jmax = _chunks_of(layer, tc * 4 + qtl)
        binfo.append((ch, jmax))

    pe0 = pb.cnt["pe"]
    act0 = pb.cnt["act"]
    dve0 = pb.cnt["dve"]
    pe_sc = {}
    pe_tr = {}
    pe_av = {}
    act_exp = {}
    act_f = {}
    act_pt = {}
    act_ao = {}
    dve_mx = {}
    dve_gm = {}
    dve_rs = {}
    pe_c = pe0
    act_c = act0
    dve_c = dve0
    cg_of = {}
    cg = 0
    pe_tc_done = {}
    for i in range(nblk + DP):
        if i < nblk:
            for kb in range(len(binfo[i][0])):
                pe_c += 1
                pe_sc[(i, kb)] = pe_c
                cg_of[(i, kb)] = cg
                cg += 1
            tc, h, qtl = blocks[i]
            if (h, qtl) == (15, 3):
                pe_tc_done[tc] = pe_c
        j = i - DP
        if j >= 0:
            pe_c += 1
            pe_tr[j] = pe_c
            pe_c += 1
            pe_av[j] = pe_c
    for i in range(nblk + 1):
        if i < nblk:
            for kb in range(len(binfo[i][0])):
                dve_c += 1
                dve_mx[(i, kb)] = dve_c
            dve_c += 1
            dve_gm[i] = dve_c
        if i >= 1:
            dve_c += 3
            dve_rs[i - 1] = dve_c
    for i in range(nblk):
        for kb in range(len(binfo[i][0])):
            act_c += 1
            act_exp[(i, kb)] = act_c
        act_c += 1
        act_f[i] = act_c
        j = i - DP
        if j >= 0:
            act_c += 1
            act_pt[j] = act_c
            tc, h, qtl = blocks[j]
            if qtl == 3:
                act_c += 1
                act_ao[(tc, h)] = act_c
    for j in range(nblk - DP, nblk):
        act_c += 1
        act_pt[j] = act_c
        tc, h, qtl = blocks[j]
        if qtl == 3:
            act_c += 1
            act_ao[(tc, h)] = act_c

    cg_inv = {v: k for k, v in cg_of.items()}

    @pb.blk.sync
    def _(sy):
        pb.dma(sy, iden[:, :], iden_d)
        pb.dma(sy, tri[:, :], tri_d)
        for g in range(nh_k):
            pb.dma(sy, k_sb[:, g * T:(g + 1) * T], k_d[g])
        for tt in range(NTT):
            pb.dma(sy, v_sb[:, tt * vdw:(tt + 1) * vdw], v_d[tt])
    base_loads = pb.cnt["sq"]

    q_toks = []

    @pb.blk.gpsimd
    def _(g):
        for tc in range(NTC):
            if tc >= 1:
                g.wait_ge(pb.sem["pe"], pe_tc_done[tc - 1])
            for h in range(16):
                tok = pb.dma(g, q_sb[:, h * TCW:(h + 1) * TCW], q_d[tc, h],
                             sem="qld")
            q_toks.append(tok)

    @pb.blk.tensor
    def _(te):
        te.wait_ge(pb.sem["sq"], base_loads)
        for i in range(nblk + DP):
            if i < nblk:
                tc, h, qtl = blocks[i]
                g = h // 4 if layer == 0 else h
                chs, jmax = binfo[i]
                if h == 0 and qtl == 0:
                    te.wait_ge(pb.sem_get("qld"), q_toks[tc])
                lhsq = q_sb[:, h * TCW + qtl * P: h * TCW + (qtl + 1) * P]
                for kb, (lo, hi, rel, hasdiag) in enumerate(chs):
                    mycg = cg_of[(i, kb)]
                    if mycg >= 4:
                        pi, pkb = cg_inv[mycg - 4]
                        te.wait_ge(pb.sem["act"], act_exp[(pi, pkb)])
                    bank = S_ps[mycg % 4]
                    for (rlo, n, klo) in rel:
                        ins = te.matmul(bank[:, rlo:rlo + n], lhsq,
                                        k_sb[:, g * T + klo: g * T + klo + n],
                                        start=True, stop=True)
                    if hasdiag:
                        w = hi - lo
                        ins = te.matmul(bank[:, w - P:w], iden[:, :], tri[:, :],
                                        start=False, stop=True)
                    assert pb.inc(ins, "pe") == pe_sc[(i, kb)]
            j = i - DP
            if j >= 0:
                tc, h, qtl = blocks[j]
                g = h // 4 if layer == 0 else h
                chs, jmax = binfo[j]
                nkc = jmax // P
                te.wait_ge(pb.sem["dve"], dve_rs[j])
                if j >= 1:
                    te.wait_ge(pb.sem["act"], act_pt[j - 1])
                for kc in range(nkc):
                    ins = te.transpose(PT_ps[:, kc * P:(kc + 1) * P],
                                       P2[j % 2][:, kc * P:(kc + 1) * P], iden[:, :])
                assert pb.inc(ins, "pe") == pe_tr[j]
                te.wait_ge(pb.sem["act"], act_pt[j])
                if qtl == 0:
                    pk = (tc, h - 1) if h > 0 else ((tc - 1, 15) if tc > 0 else None)
                    if pk is not None:
                        te.wait_ge(pb.sem["act"], act_ao[pk])
                for kc in range(nkc):
                    vt = (0 if kc == 0 else kc + 7) if layer == 0 else kc
                    vcol = (g if layer == 0 else h) * P
                    ins = te.matmul(av_ps[:, qtl * P:(qtl + 1) * P],
                                    v_sb[:, vt * vdw + vcol: vt * vdw + vcol + P],
                                    PT[:, kc * P:(kc + 1) * P],
                                    start=(kc == 0), stop=(kc == nkc - 1))
                assert pb.inc(ins, "pe") == pe_av[j]

    def _combine_rest(v, i):
        chs, jmax = binfo[i]
        KB = len(chs)
        b2 = i % 2
        if KB > 1:
            v.wait_ge(pb.sem["act"], act_f[i])
            ins = v.scalar_tensor_tensor(ssc[:, 0:KB], fa[b2][:, 0:KB], 1.0,
                                         sa[b2][:, 0:KB], ALU.mult, ALU.mult,
                                         accum_out=stot[:, 0:1])
            v.wait_ge(pb.sem["dve"], pb.inc(ins, "dve"))
            ins = v.reciprocal(rdiv[:, 0:1], stot[:, 0:1])
        else:
            v.wait_ge(pb.sem["act"], act_exp[(i, 0)])
            ins = v.reciprocal(rdiv[:, 0:1], sa[b2][:, 0:1])
            pb.inc(ins, "dve")
            v.wait_ge(pb.sem["dve"], pb.cnt["dve"])
            ins = v.tensor_copy(ssc[:, 0:1], rdiv[:, 0:1])
        v.wait_ge(pb.sem["dve"], pb.inc(ins, "dve"))
        if i >= 2:
            v.wait_ge(pb.sem["pe"], pe_tr[i - 2])
        for kb, (lo, hi, rel, hasdiag) in enumerate(chs):
            if KB > 1:
                ins = v.tensor_scalar(P2[b2][:, lo:hi], Pm[b2][:, lo:hi],
                                      fa[b2][:, kb:kb + 1], rdiv[:, 0:1],
                                      ALU.mult, ALU.mult)
            else:
                ins = v.tensor_scalar_mul(P2[b2][:, lo:hi], Pm[b2][:, lo:hi],
                                          rdiv[:, 0:1])
        assert pb.inc(ins, "dve") == dve_rs[i]

    @pb.blk.vector
    def _(v):
        for i in range(nblk + 1):
            if i < nblk:
                chs, jmax = binfo[i]
                KB = len(chs)
                b2 = i % 2
                if i >= 2:
                    v.wait_ge(pb.sem["act"], act_f[i - 2])
                for kb, (lo, hi, rel, hasdiag) in enumerate(chs):
                    v.wait_ge(pb.sem["pe"], pe_sc[(i, kb)])
                    bank = S_ps[cg_of[(i, kb)] % 4]
                    w = hi - lo
                    ins = v.reduce_max(nm[b2][:, kb:kb + 1], bank[:, 0:w], AX.X,
                                       negate=True)
                    assert pb.inc(ins, "dve") == dve_mx[(i, kb)]
                v.wait_ge(pb.sem["dve"], dve_mx[(i, KB - 1)])
                ins = v.tensor_reduce(ngm[b2][:, 0:1], nm[b2][:, 0:KB], AX.X,
                                      ALU.min)
                assert pb.inc(ins, "dve") == dve_gm[i]
            if i >= 1:
                _combine_rest(v, i - 1)

    pb.sem_get("aos")
    aos0 = pb.cnt["aos"]

    @pb.blk.scalar
    def _(sc):
        for i in range(nblk):
            chs, jmax = binfo[i]
            KB = len(chs)
            b2 = i % 2
            tc, h, qtl = blocks[i]
            if i >= 2:
                sc.wait_ge(pb.sem["dve"], dve_rs[i - 2])
            for kb, (lo, hi, rel, hasdiag) in enumerate(chs):
                sc.wait_ge(pb.sem["dve"], dve_mx[(i, kb)])
                bank = S_ps[cg_of[(i, kb)] % 4]
                ins = sc.activation(Pm[b2][:, lo:hi], bank[:, 0:hi - lo], AF.Exp,
                                    bias=nm[b2][:, kb:kb + 1],
                                    accum_out=sa[b2][:, kb:kb + 1])
                assert pb.inc(ins, "act") == act_exp[(i, kb)]
            sc.wait_ge(pb.sem["dve"], dve_gm[i])
            ins = sc.activation(fa[b2][:, 0:KB], nm[b2][:, 0:KB], AF.Exp,
                                bias=ngm[b2][:, 0:1], scale=-1.0)
            assert pb.inc(ins, "act") == act_f[i]
            j = i - DP
            if j >= 0:
                _emit_ptao(pb, sc, j, blocks, binfo, PT, PT_ps, ao, av_ps,
                           pe_tr, pe_av, act_pt, act_ao, aos0, base_loads)
        for j in range(nblk - DP, nblk):
            _emit_ptao(pb, sc, j, blocks, binfo, PT, PT_ps, ao, av_ps,
                       pe_tr, pe_av, act_pt, act_ao, aos0, base_loads)

    @pb.blk.sync
    def _(sy):
        for tc in range(NTC):
            sy.wait_ge(pb.sem["act"], act_ao[(tc, 15)])
            pb.dma(sy, ao_d[tc], ao[:, :], sem="aos")


def _emit_ptao(pb, sc, j, blocks, binfo, PT, PT_ps, ao, av_ps,
               pe_tr, pe_av, act_pt, act_ao, aos0, base_loads):
    tc, h, qtl = blocks[j]
    chs, jmax = binfo[j]
    nkc = jmax // P
    sc.wait_ge(pb.sem["pe"], pe_tr[j])
    ins = sc.copy(PT[:, 0:nkc * P], PT_ps[:, 0:nkc * P])
    assert pb.inc(ins, "act") == act_pt[j]
    if qtl == 3:
        if h == 0 and tc >= 1:
            sc.wait_ge(pb.sem_get("aos"), aos0 + 16 * tc)
        sc.wait_ge(pb.sem["pe"], pe_av[j])
        ins = sc.copy(ao[:, h * TCW:(h + 1) * TCW], av_ps[:, :])
        assert pb.inc(ins, "act") == act_ao[(tc, h)]


def build_attncore(layer):
    nc = bass.Bass("TRN2", target_bir_lowering=False)
    es = ExitStack()
    io = io_attn(nc, layer)
    io["iden"] = nc.dram_tensor("iden", [P, P], BF16, kind="ExternalInput").ap()
    io["tri"] = nc.dram_tensor("tri", [P, P], BF16, kind="ExternalInput").ap()
    with es:
        pb = PB(nc, es, npb=0, nwb=0)
        emit_attn2(pb, layer, io)

        @pb.blk.sync
        def _(sy):
            pb.wait_all_dma(sy)
    return nc



def emit_wo(pb, layer, io):
    ao_d, wo_d, x_d, xo_d = io["ao"], io["wo"], io["x"], io["xo"]
    if layer == 1:
        bo_d = io["bo"]
    if True:
        pb.alloc_pool(4, 4)
        ao = [pb.sb(f"ao_s{i}", [P, 16 * TCW], BF16) for i in range(2)]
        xt = [pb.sb(f"xt{i}", [P, TCW], F32) for i in range(4)]
        xo = [pb.sb(f"xo{i}", [P, TCW], F32) for i in range(2)]
        if layer == 1:
            bo_sb = pb.sb("bo_s", [P, 16], F32)

            @pb.blk.sync
            def _(sy):
                pb.dma(sy, bo_sb[:, :], bo_d)
            pb.barrier()

        gq_toks = []

        @pb.blk.gpsimd
        def _(g):
            for sl in range(4):
                gq_toks.append(pb.dma(g, pb.wbuf[sl][:, :], wo_d[sl], sem=f"w{sl}"))

        pe_g0 = pb.cnt["pe"]
        dve_g0 = pb.cnt["dve"]
        ao_toks = []

        @pb.blk.gpsimd
        def _(g):
            for tc in range(NTC):
                pb.wait(g, "pe", pe_g0 + 16 * (tc - 1))
                ao_toks.append(pb.dma(g, ao[tc % 2][:, :], ao_d[tc],
                                      sem=f"aold{tc % 2}"))

        @pb.blk.tensor
        def _(te):
            for sl in range(4):
                te.wait_ge(pb.sem_get(f"w{sl}"), gq_toks[sl])
            for tc in range(NTC):
                te.wait_ge(pb.sem_get(f"aold{tc % 2}"), ao_toks[tc])
                for gi in range(16):
                    sl, osub = gi // 4, gi % 4
                    G = tc * 16 + gi
                    bank = pb.pbank[G % 4]
                    if G >= 4:
                        te.wait_ge(pb.sem["dve"], dve_g0 + G - 3)
                    for c in range(DC):
                        ins = te.matmul(bank[:, :],
                                        pb.wbuf[sl][:, c * TCW + osub * P:
                                                    c * TCW + osub * P + P],
                                        ao[tc % 2][:, c * TCW:(c + 1) * TCW],
                                        start=(c == 0), stop=(c == DC - 1))
                    pb.inc(ins, "pe")

        if layer == 1:
            def dve_op(v, g, xo_t, bank, xt_t):
                return v.scalar_tensor_tensor(xo_t[:, :], bank[:, :],
                                              bo_sb[:, g % 16:g % 16 + 1], xt_t[:, :],
                                              ALU.add, ALU.add)
        else:
            def dve_op(v, g, xo_t, bank, xt_t):
                return v.tensor_tensor(xo_t[:, :], bank[:, :], xt_t[:, :], ALU.add)

        emit_resid_stream(pb, 4 * 16, pe_g0, lambda g: pb.pbank[g % 4],
                          lambda g: x_d[g % 16][:, (g // 16) * TCW:(g // 16 + 1) * TCW],
                          lambda g: xo_d[g % 16][:, (g // 16) * TCW:(g // 16 + 1) * TCW],
                          dve_op, xt, xo)


def build_woresid(layer):
    nc = bass.Bass("TRN2", target_bir_lowering=False)
    es = ExitStack()
    io = {}
    io["ao"] = nc.dram_tensor("ao", [NTC, P, 16 * TCW], BF16, kind="ExternalInput").ap()
    io["wo"] = nc.dram_tensor("wo", [4, P, DC * TCW], F32, kind="ExternalInput").ap()
    io["x"] = nc.dram_tensor("x", [DC, P, T], F32, kind="ExternalInput").ap()
    io["xo"] = nc.dram_tensor("xo", [DC, P, T], F32, kind="ExternalOutput").ap()
    if layer == 1:
        io["bo"] = nc.dram_tensor("bo", [P, 16], F32, kind="ExternalInput").ap()
    with es:
        pb = PB(nc, es, npb=0, nwb=0)
        emit_wo(pb, layer, io)

        @pb.blk.sync
        def _(sy):
            pb.wait_all_dma(sy)
    return nc



def emit_ffn(pb, layer, io):
    x_d, xo_d = io["x"], io["xo"]
    nic = IC0 if layer == 0 else IC1
    ngs = nic // 4
    nds = DC
    wg_d, wu_d, wd_d = io["wg"], io["wu"], io["wd"]
    h_scr = io["hscr"]
    onesb_d, onesrow_d = io["c_onesb"], io["c_onesrow"]
    if layer == 1:
        gs_d, us_d, ds_d = io["gs"], io["us"], io["ds"]
    if True:
        pb.alloc_pool(4, 4)
        lnb = mk_ln_bufs(pb, ping=False)
        xn = pb.sb("xn", [P, DC * T], BF16)
        sg = pb.sb("sg", [P, TCW], BF16)
        hq = [pb.sb(f"hq{i}", [P, 4 * TCW], BF16) for i in range(2)]
        xt = [pb.sb(f"xt{i}", [P, TCW], F32) for i in range(2)]
        xo = [pb.sb(f"xo{i}", [P, TCW], F32) for i in range(2)]
        if layer == 0:
            mt = [pb.sb(f"m24_{i}", [P, TCW], BF16) for i in range(10)]
        if layer == 1:
            gs_sb = pb.sb("gs_s", [P, nic], F32)
            us_sb = pb.sb("us_s", [P, nic], F32)
            ds_sb = pb.sb("ds_s", [P, DC], F32)

        @pb.blk.sync
        def _(sy):
            pb.dma(sy, lnb["onesb"][:, 0:1], onesb_d)
            pb.dma(sy, lnb["ones_row"][0:1, :], onesrow_d)
            if layer == 1:
                pb.dma(sy, gs_sb[:, :], gs_d)
                pb.dma(sy, us_sb[:, :], us_d)
                pb.dma(sy, ds_sb[:, :], ds_d)
        pb.barrier()

        emit_ln(pb, x_d, lnb, list(range(NTC)),
                lambda c, tc: xn[:, c * T + tc * TCW: c * T + (tc + 1) * TCW])
        pb.barrier()

        pe0 = pb.cnt["pe"]
        act0 = pb.cnt["act"]
        dve0 = pb.cnt["dve"]
        pb.sem_get("hst0")
        pb.sem_get("hst1")
        hst0p = [pb.cnt["hst0"], pb.cnt["hst1"]]
        gq_toks = []
        nst = ngs * NTC
        GPI = 5 if layer == 0 else 4

        def evac_tok(j):
            return dve0 + (j // 4) * GPI + (j % 4) + 1

        @pb.blk.gpsimd
        def _(g):
            for sl in range(ngs):
                if sl >= 2:
                    g.wait_ge(pb.sem["pe"], pe0 + 32 * (sl - 1))
                t1 = pb.dma(g, pb.wbuf[(2 * sl) % 4][:, :], wg_d[sl],
                            sem=f"w{(2 * sl) % 4}")
                t2 = pb.dma(g, pb.wbuf[(2 * sl + 1) % 4][:, :], wu_d[sl],
                            sem=f"w{(2 * sl + 1) % 4}")
                gq_toks.append((t1, t2))

        @pb.blk.tensor
        def _(te):
            for sl in range(ngs):
                te.wait_ge(pb.sem_get(f"w{(2 * sl) % 4}"), gq_toks[sl][0])
                te.wait_ge(pb.sem_get(f"w{(2 * sl + 1) % 4}"), gq_toks[sl][1])
                for tc in range(NTC):
                    for osub in range(4):
                        k = (sl * NTC + tc) * 4 + osub
                        gb = pb.pbank[(2 * k) % 4]
                        ub = pb.pbank[(2 * k + 1) % 4]
                        pb.wait(te, "act", act0 + k - 1)
                        if k >= 2:
                            te.wait_ge(pb.sem["dve"], evac_tok(k - 2))
                        for c in range(DC):
                            ins = te.matmul(gb[:, :],
                                            pb.wbuf[(2 * sl) % 4][:, c * TCW + osub * P:
                                                                  c * TCW + osub * P + P],
                                            xn[:, c * T + tc * TCW: c * T + (tc + 1) * TCW],
                                            start=(c == 0), stop=(c == DC - 1))
                        pb.inc(ins, "pe")
                        for c in range(DC):
                            ins = te.matmul(ub[:, :],
                                            pb.wbuf[(2 * sl + 1) % 4][:, c * TCW + osub * P:
                                                                      c * TCW + osub * P + P],
                                            xn[:, c * T + tc * TCW: c * T + (tc + 1) * TCW],
                                            start=(c == 0), stop=(c == DC - 1))
                        pb.inc(ins, "pe")

        @pb.blk.scalar
        def _(sc):
            for k in range(4 * nst):
                sl = k // 16
                sc.wait_ge(pb.sem["pe"], pe0 + 2 * k + 1)
                if k >= 1:
                    sc.wait_ge(pb.sem["dve"], evac_tok(k - 1))
                gb = pb.pbank[(2 * k) % 4]
                if layer == 1:
                    ic = (k // 4) // NTC * 4 + k % 4
                    ins = sc.activation(sg[:, :], gb[:, :], AF.Silu,
                                        scale=gs_sb[:, ic:ic + 1])
                else:
                    ins = sc.activation(sg[:, :], gb[:, :], AF.Silu)
                pb.inc(ins, "act")

        @pb.blk.vector
        def _(v):
            st_idx = 0
            for st in range(nst):
                sl, tc = st // NTC, st % NTC
                buf = hq[st % 2]
                if st >= 2:
                    v.wait_ge(pb.sem[f"hst{st % 2}"],
                              hst0p[st % 2] + 16 * ((st - 2) // 2 + 1))
                for osub in range(4):
                    k = st * 4 + osub
                    ic = sl * 4 + osub
                    v.wait_ge(pb.sem["pe"], pe0 + 2 * k + 2)
                    v.wait_ge(pb.sem["act"], act0 + k + 1)
                    ub = pb.pbank[(2 * k + 1) % 4]
                    dst = buf[:, osub * TCW:(osub + 1) * TCW]
                    if layer == 1:
                        ins = v.scalar_tensor_tensor(dst, ub[:, :], us_sb[:, ic:ic + 1],
                                                     sg[:, :], ALU.mult, ALU.mult)
                    else:
                        ins = v.tensor_tensor(dst, ub[:, :], sg[:, :], ALU.mult)
                    pb.inc(ins, "dve")
                if layer == 0:
                    a0, a1, a2, a3 = mt[0:4]
                    c01, c02, c03, c12, c13, c23 = mt[4:10]
                    A = [buf[:, kk * TCW:(kk + 1) * TCW] for kk in range(4)]
                    for kk, dstt in enumerate((a0, a1, a2, a3)):
                        v.tensor_single_scalar(dstt[:, :].bitcast(mybir.dt.uint16),
                                               A[kk].bitcast(mybir.dt.uint16),
                                               0x7FFF, ALU.bitwise_and)
                    v.tensor_tensor(c01[:, :], a0[:, :], a1[:, :], ALU.is_ge)
                    v.tensor_tensor(c02[:, :], a0[:, :], a2[:, :], ALU.is_ge)
                    v.tensor_tensor(c03[:, :], a0[:, :], a3[:, :], ALU.is_ge)
                    v.tensor_tensor(c12[:, :], a1[:, :], a2[:, :], ALU.is_ge)
                    v.tensor_tensor(c13[:, :], a1[:, :], a3[:, :], ALU.is_ge)
                    v.tensor_tensor(c23[:, :], a2[:, :], a3[:, :], ALU.is_ge)
                    v.tensor_tensor(a0[:, :], c01[:, :], c02[:, :], ALU.add)
                    v.tensor_tensor(a0[:, :], a0[:, :], c03[:, :], ALU.add)
                    v.tensor_single_scalar(a0[:, :], a0[:, :], 2.0, ALU.is_ge)
                    v.tensor_tensor(a1[:, :], c12[:, :], c13[:, :], ALU.add)
                    v.tensor_tensor(a1[:, :], a1[:, :], c01[:, :], ALU.subtract)
                    v.tensor_single_scalar(a1[:, :], a1[:, :], 1.0, ALU.is_ge)
                    v.tensor_tensor(a2[:, :], c02[:, :], c12[:, :], ALU.add)
                    v.tensor_tensor(a2[:, :], a2[:, :], c23[:, :], ALU.subtract)
                    v.tensor_single_scalar(a2[:, :], a2[:, :], 0.0, ALU.is_le)
                    v.tensor_tensor(a3[:, :], c03[:, :], c13[:, :], ALU.add)
                    v.tensor_tensor(a3[:, :], a3[:, :], c23[:, :], ALU.add)
                    v.tensor_single_scalar(a3[:, :], a3[:, :], 1.0, ALU.is_le)
                    for kk, msk in enumerate((a0, a1, a2, a3)):
                        ins = v.tensor_tensor(A[kk], A[kk], msk[:, :], ALU.mult)
                    pb.inc(ins, "dve")

        @pb.blk.sync
        def _(sy):
            for st in range(nst):
                sl, tc = st // NTC, st % NTC
                sy.wait_ge(pb.sem["dve"], dve0 + (st + 1) * GPI)
                pb.dma(sy, h_scr[sl, tc], hq[st % 2][:, :], sem=f"hst{st % 2}")

        pb.barrier()

        peC0 = pb.cnt["pe"]
        dveC0 = pb.cnt["dve"]
        for tc in range(NTC):
            pb.sem_get("hld")

            @pb.blk.sync
            def _(sy):
                pb.wait(sy, "pe", peC0 + 16 * tc)
                for sl in range(ngs):
                    pb.dma(sy, xn[:, sl * 4 * TCW:(sl + 1) * 4 * TCW],
                           h_scr[sl, tc], sem="hld")
            hld_tok = pb.cnt["hld"]

            gqd_toks = []

            @pb.blk.gpsimd
            def _(g):
                for sl in range(nds):
                    Gg = tc * nds + sl
                    if Gg >= 4:
                        g.wait_ge(pb.sem["pe"], peC0 + (Gg - 3))
                    gqd_toks.append(pb.dma(g, pb.wbuf[sl % 4][:, 0:nic * P], wd_d[sl],
                                           sem=f"w{sl % 4}"))

            @pb.blk.tensor
            def _(te):
                te.wait_ge(pb.sem["hld"], hld_tok)
                for sl in range(nds):
                    Gg = tc * nds + sl
                    te.wait_ge(pb.sem_get(f"w{sl % 4}"), gqd_toks[sl])
                    bank = pb.pbank[Gg % 4]
                    if Gg >= 4:
                        te.wait_ge(pb.sem["dve"], dveC0 + Gg - 3)
                    for c in range(nic):
                        ins = te.matmul(bank[:, :],
                                        pb.wbuf[sl % 4][:, c * P: (c + 1) * P],
                                        xn[:, c * TCW:(c + 1) * TCW],
                                        start=(c == 0), stop=(c == nic - 1))
                    pb.inc(ins, "pe")

            if layer == 1:
                def dve_op(v, g, xo_t, bank, xt_t):
                    return v.scalar_tensor_tensor(xo_t[:, :], bank[:, :],
                                                  ds_sb[:, g:g + 1], xt_t[:, :],
                                                  ALU.mult, ALU.add)
            else:
                def dve_op(v, g, xo_t, bank, xt_t):
                    return v.tensor_tensor(xo_t[:, :], bank[:, :], xt_t[:, :], ALU.add)

            emit_resid_stream(pb, nds, peC0 + 16 * tc, lambda g: pb.pbank[(tc * nds + g) % 4],
                              lambda g: x_d[g][:, tc * TCW:(tc + 1) * TCW],
                              lambda g: xo_d[g][:, tc * TCW:(tc + 1) * TCW],
                              dve_op, xt, xo)


def io_ffn(nc, layer, pfx="", scratch=()):
    nic = IC0 if layer == 0 else IC1
    ngs = nic // 4
    kind = lambda n, out: "Internal" if n in scratch else (
        "ExternalOutput" if out else "ExternalInput")
    io = {}
    io["x"] = nc.dram_tensor(pfx + "x", [DC, P, T], F32, kind=kind("x", False)).ap()
    io["xo"] = nc.dram_tensor(pfx + "xo", [DC, P, T], F32, kind=kind("xo", True)).ap()
    io["wg"] = nc.dram_tensor(pfx + "wg", [ngs, P, DC * TCW], F32, kind="ExternalInput").ap()
    io["wu"] = nc.dram_tensor(pfx + "wu", [ngs, P, DC * TCW], F32, kind="ExternalInput").ap()
    io["wd"] = nc.dram_tensor(pfx + "wd", [DC, P, nic * P], F32, kind="ExternalInput").ap()
    io["hscr"] = nc.dram_tensor(pfx + "hscr", [ngs, NTC, P, 4 * TCW], BF16,
                                kind="Internal").ap()
    if layer == 1:
        io["gs"] = nc.dram_tensor(pfx + "gs", [P, nic], F32, kind="ExternalInput").ap()
        io["us"] = nc.dram_tensor(pfx + "us", [P, nic], F32, kind="ExternalInput").ap()
        io["ds"] = nc.dram_tensor(pfx + "ds", [P, DC], F32, kind="ExternalInput").ap()
    return io


def build_ffn(layer):
    nc = bass.Bass("TRN2", target_bir_lowering=False)
    es = ExitStack()
    io = io_ffn(nc, layer)
    io.update(_io_consts(nc, rope=False))
    with es:
        pb = PB(nc, es, npb=0, nwb=0)
        emit_ffn(pb, layer, io)

        @pb.blk.sync
        def _(sy):
            pb.wait_all_dma(sy)
    return nc



def build_fused():
    nc = bass.Bass("TRN2", target_bir_lowering=False)
    es = ExitStack()

    consts = _io_consts(nc, rope=True)
    consts["iden"] = nc.dram_tensor("iden", [P, P], BF16, kind="ExternalInput").ap()
    consts["tri"] = nc.dram_tensor("tri", [P, P], BF16, kind="ExternalInput").ap()

    x_in = nc.dram_tensor("x", [DC, P, T], F32, kind="ExternalInput").ap()
    x_out = nc.dram_tensor("xo", [DC, P, T], F32, kind="ExternalOutput").ap()
    x1 = nc.dram_tensor("x1", [DC, P, T], F32, kind="Internal").ap()
    x2 = nc.dram_tensor("x2", [DC, P, T], F32, kind="Internal").ap()
    x3 = nc.dram_tensor("x3", [DC, P, T], F32, kind="Internal").ap()

    ioq0 = io_qkv(nc, 0, pfx="a0_", scratch=("x", "q", "k", "v"))
    ioq0["x"] = x_in
    ioq0.update(consts)
    ioa0 = {k: ioq0[k] for k in ("q", "k", "v")}
    ioa0["ao"] = nc.dram_tensor("a0_ao", [NTC, P, 16 * TCW], BF16, kind="Internal").ap()
    ioa0.update(consts)
    iow0 = {"ao": ioa0["ao"], "x": x_in, "xo": x1,
            "wo": nc.dram_tensor("a0_wo", [4, P, DC * TCW], F32, kind="ExternalInput").ap()}
    iof0 = io_ffn(nc, 0, pfx="f0_", scratch=("x", "xo"))
    iof0["x"] = x1
    iof0["xo"] = x2
    iof0.update(consts)

    ioq1 = io_qkv(nc, 1, pfx="a1_", scratch=("x", "q", "k", "v"))
    ioq1["x"] = x2
    ioq1.update(consts)
    ioa1 = {k: ioq1[k] for k in ("q", "k", "v")}
    ioa1["ao"] = nc.dram_tensor("a1_ao", [NTC, P, 16 * TCW], BF16, kind="Internal").ap()
    ioa1.update(consts)
    iow1 = {"ao": ioa1["ao"], "x": x2, "xo": x3,
            "wo": nc.dram_tensor("a1_wo", [4, P, DC * TCW], F32, kind="ExternalInput").ap(),
            "bo": nc.dram_tensor("a1_bo", [P, 16], F32, kind="ExternalInput").ap()}
    iof1 = io_ffn(nc, 1, pfx="f1_", scratch=("x", "xo"))
    iof1["x"] = x3
    iof1["xo"] = x_out
    iof1.update(consts)

    phases = [("q0_", emit_qkv, 0, ioq0), ("t0_", emit_attn2, 0, ioa0),
              ("w0_", emit_wo, 0, iow0), ("m0_", emit_ffn, 0, iof0),
              ("q1_", emit_qkv, 1, ioq1), ("t1_", emit_attn2, 1, ioa1),
              ("w1_", emit_wo, 1, iow1), ("m1_", emit_ffn, 1, iof1)]

    with es:
        pb = PB(nc, es, npb=0, nwb=0)
        for pfx, emit, layer, io in phases:
            pb.push(pfx)
            with nc.named_scope(pfx):
                emit(pb, layer, io)
                pb.barrier()
            pb.pop()

        @pb.blk.sync
        def _(sy):
            pb.wait_all_dma(sy)
    return nc



_CACHE = {}


def _consts_host():
    cos, sin = rope_tables()
    return {
        "c_onesb": np.ones((P, 1), dtype=BF),
        "c_onesrow": np.ones((1, P), dtype=np.float32),
        "c_cos": (np.concatenate([cos, cos], axis=0) * math.sqrt(ISQ)).astype(BF),
        "c_sin": (np.concatenate([sin, sin], axis=0) * math.sqrt(ISQ)).astype(BF),
        "iden": np.eye(P, dtype=BF),
        "tri": np.triu(np.full((P, P), NEG, dtype=np.float32), 1).astype(BF),
    }


def _cget(key, fn):
    if key not in _CACHE:
        _CACHE[key] = fn()
    return _CACHE[key]


def _kernel_unfused(**inp):
    ch = _consts_host()
    rp = _rope_perm()
    p24 = _perm24()

    wq0 = prep_w(col_perm_heads(np.asarray(inp["L0_attn_Wq"], np.float32), rp), TCW)
    wk0 = prep_w(col_perm_heads(np.asarray(inp["L0_attn_Wk"], np.float32), rp), TCW)
    wv0 = prep_w(np.asarray(inp["L0_attn_Wv"], np.float32), TCW)[0]
    wo0 = prep_w(np.asarray(inp["L0_attn_Wo"], np.float32), TCW)
    wg0 = prep_w(np.asarray(inp["L0_ffn_w_gate"], np.float32)[:, p24], TCW)
    wu0 = prep_w(np.asarray(inp["L0_ffn_w_up"], np.float32)[:, p24], TCW)
    wd0 = prep_w(np.asarray(inp["L0_ffn_w_down"], np.float32)[p24, :], P)

    wq1 = prep_w(col_perm_heads(np.asarray(inp["L1_attn_Wq"], np.float32).T, rp), TCW)
    wk1 = prep_w(col_perm_heads(np.asarray(inp["L1_attn_Wk"], np.float32).T, rp), TCW)
    wv1 = prep_w(np.ascontiguousarray(np.asarray(inp["L1_attn_Wv"], np.float32).T), TCW)
    wo1 = prep_w(np.ascontiguousarray(np.asarray(inp["L1_attn_Wo"], np.float32).T), TCW)
    bq1 = bias_cols(col_perm_heads(np.asarray(inp["L1_attn_bq"], np.float32)[None, :], rp)[0])
    bk1 = bias_cols(col_perm_heads(np.asarray(inp["L1_attn_bk"], np.float32)[None, :], rp)[0])
    bv1 = np.ascontiguousarray(np.asarray(inp["L1_attn_bv"], np.float32)[None, :])
    bo1 = bias_cols(np.asarray(inp["L1_attn_bo"], np.float32))
    wg1 = prep_w(np.ascontiguousarray(np.asarray(inp["L1_ffn_gate_w"], np.float32).T), TCW)
    wu1 = prep_w(np.ascontiguousarray(np.asarray(inp["L1_ffn_up_w"], np.float32).T), TCW)
    wd1 = prep_w(np.ascontiguousarray(np.asarray(inp["L1_ffn_down_w"], np.float32).T), P)
    gs1 = bias_cols(np.asarray(inp["L1_ffn_gate_s"], np.float32))
    us1 = bias_cols(np.asarray(inp["L1_ffn_up_s"], np.float32))
    ds1 = bias_cols(np.asarray(inp["L1_ffn_down_s"], np.float32))

    x0 = np.asarray(inp["L0_attn_x"], np.float32)
    x_fm = [np.ascontiguousarray(x0[b].T).reshape(DC, P, T) for b in range(NCORES)]

    core_ids = list(range(NCORES))

    def run(nc, per_core, shared):
        in_maps = [dict(shared, **pc) for pc in per_core]
        return run_bass_kernel_spmd(nc, in_maps, core_ids=core_ids).results

    cc = {k: ch[k] for k in ("c_onesb", "c_onesrow", "c_cos", "c_sin")}
    cl = {k: ch[k] for k in ("c_onesb", "c_onesrow")}

    r = run(_cget("qkv0", lambda: build_qkv(0)), [{"x": x} for x in x_fm],
            dict(cc, wq=wq0, wk=wk0, wv=wv0))
    qkv = [(r[b]["q"], r[b]["k"], r[b]["v"]) for b in range(NCORES)]

    r = run(_cget("attn0", lambda: build_attncore(0)),
            [{"q": q, "k": k, "v": v} for (q, k, v) in qkv],
            {"iden": ch["iden"], "tri": ch["tri"]})
    aos = [r[b]["ao"] for b in range(NCORES)]

    r = run(_cget("wo0", lambda: build_woresid(0)),
            [{"ao": aos[b], "x": x_fm[b]} for b in range(NCORES)], {"wo": wo0})
    x1 = [r[b]["xo"] for b in range(NCORES)]

    r = run(_cget("ffn0", lambda: build_ffn(0)), [{"x": x1[b]} for b in range(NCORES)],
            dict(cl, wg=wg0, wu=wu0, wd=wd0))
    x2 = [r[b]["xo"] for b in range(NCORES)]

    r = run(_cget("qkv1", lambda: build_qkv(1)), [{"x": x2[b]} for b in range(NCORES)],
            dict(cc, wq=wq1, wk=wk1, wv=wv1, bq=bq1, bk=bk1, bv=bv1))
    qkv = [(r[b]["q"], r[b]["k"], r[b]["v"]) for b in range(NCORES)]

    r = run(_cget("attn1", lambda: build_attncore(1)),
            [{"q": q, "k": k, "v": v} for (q, k, v) in qkv],
            {"iden": ch["iden"], "tri": ch["tri"]})
    aos = [r[b]["ao"] for b in range(NCORES)]

    r = run(_cget("wo1", lambda: build_woresid(1)),
            [{"ao": aos[b], "x": x2[b]} for b in range(NCORES)],
            {"wo": wo1, "bo": bo1})
    x3 = [r[b]["xo"] for b in range(NCORES)]

    r = run(_cget("ffn1", lambda: build_ffn(1)), [{"x": x3[b]} for b in range(NCORES)],
            dict(cl, wg=wg1, wu=wu1, wd=wd1, gs=gs1, us=us1, ds=ds1))
    x4 = [r[b]["xo"] for b in range(NCORES)]

    out = np.empty((NCORES, T, D), dtype=np.float32)
    for b in range(NCORES):
        out[b] = x4[b].reshape(D, T).T
    return out


def _prep_all(inp):
    ch = _consts_host()
    rp = _rope_perm()
    p24 = _perm24()
    w = {}
    w["a0_wq"] = prep_w(col_perm_heads(np.asarray(inp["L0_attn_Wq"], np.float32), rp), TCW)
    w["a0_wk"] = prep_w(col_perm_heads(np.asarray(inp["L0_attn_Wk"], np.float32), rp), TCW)
    w["a0_wv"] = prep_w(np.asarray(inp["L0_attn_Wv"], np.float32), TCW)[0]
    w["a0_wo"] = prep_w(np.asarray(inp["L0_attn_Wo"], np.float32), TCW)
    w["f0_wg"] = prep_w(np.asarray(inp["L0_ffn_w_gate"], np.float32)[:, p24], TCW)
    w["f0_wu"] = prep_w(np.asarray(inp["L0_ffn_w_up"], np.float32)[:, p24], TCW)
    w["f0_wd"] = prep_w(np.asarray(inp["L0_ffn_w_down"], np.float32)[p24, :], P)
    w["a1_wq"] = prep_w(col_perm_heads(np.asarray(inp["L1_attn_Wq"], np.float32).T, rp), TCW)
    w["a1_wk"] = prep_w(col_perm_heads(np.asarray(inp["L1_attn_Wk"], np.float32).T, rp), TCW)
    w["a1_wv"] = prep_w(np.ascontiguousarray(np.asarray(inp["L1_attn_Wv"], np.float32).T), TCW)
    w["a1_wo"] = prep_w(np.ascontiguousarray(np.asarray(inp["L1_attn_Wo"], np.float32).T), TCW)
    w["a1_bq"] = bias_cols(col_perm_heads(np.asarray(inp["L1_attn_bq"], np.float32)[None, :], rp)[0])
    w["a1_bk"] = bias_cols(col_perm_heads(np.asarray(inp["L1_attn_bk"], np.float32)[None, :], rp)[0])
    w["a1_bv"] = np.ascontiguousarray(np.asarray(inp["L1_attn_bv"], np.float32)[None, :])
    w["a1_bo"] = bias_cols(np.asarray(inp["L1_attn_bo"], np.float32))
    w["f1_wg"] = prep_w(np.ascontiguousarray(np.asarray(inp["L1_ffn_gate_w"], np.float32).T), TCW)
    w["f1_wu"] = prep_w(np.ascontiguousarray(np.asarray(inp["L1_ffn_up_w"], np.float32).T), TCW)
    w["f1_wd"] = prep_w(np.ascontiguousarray(np.asarray(inp["L1_ffn_down_w"], np.float32).T), P)
    w["f1_gs"] = bias_cols(np.asarray(inp["L1_ffn_gate_s"], np.float32))
    w["f1_us"] = bias_cols(np.asarray(inp["L1_ffn_up_s"], np.float32))
    w["f1_ds"] = bias_cols(np.asarray(inp["L1_ffn_down_s"], np.float32))
    for k in ("c_onesb", "c_onesrow", "c_cos", "c_sin", "iden", "tri"):
        w[k] = ch[k]
    return w


def kernel(**inp):
    shared = _prep_all(inp)
    x0 = np.asarray(inp["L0_attn_x"], np.float32)
    x_fm = [np.ascontiguousarray(x0[b].T).reshape(DC, P, T) for b in range(NCORES)]
    nc = _cget("fused", build_fused)
    in_maps = [dict(shared, x=x_fm[b]) for b in range(NCORES)]
    res = run_bass_kernel_spmd(nc, in_maps, core_ids=list(range(NCORES)))
    out = np.empty((NCORES, T, D), dtype=np.float32)
    for b in range(NCORES):
        out[b] = res.results[b]["xo"].reshape(D, T).T
    return out
